# version 14
# baseline (speedup 1.0000x reference)
import os
import numpy as np
import ml_dtypes
from contextlib import ExitStack

import concourse.bass as bass
import concourse.mybir as mybir
from concourse.alu_op_type import AluOpType
from concourse.bass_utils import run_bass_kernel_spmd

F32 = mybir.dt.float32
BF16 = mybir.dt.bfloat16
I32 = mybir.dt.int32
AF = mybir.ActivationFunctionType
AX = mybir.AxisListType.X
Alu = AluOpType
bf = ml_dtypes.bfloat16

B, L, D, H, DH, II = 16, 2048, 1024, 16, 64, 2816
NCORE = 8
NSEQ = 2
T = NSEQ * L
NHC = 8
NDC = 8
NIC = 22
NTT = T // 512

ENGS = ("tensor", "vector", "scalar", "gpsimd", "sync")


class P:

    VDRAIN = bool(int(os.environ.get("KVDRAIN", "0")))

    def __init__(self, nc):
        self.nc = nc
        self.steps = {e: [] for e in ENGS}
        self.cnt = {}
        self.semh = {}
        self.last_wait = {e: {} for e in ENGS}
        self.es = ExitStack()
        self.nphase = 0

    def sem(self, name):
        self.semh[name] = self.es.enter_context(self.nc.semaphore(name))
        self.cnt[name] = 0

    serialize = False

    def ev(self, eng, fn, waits=(), inc=None, n=1):
        for s, v in waits:
            assert v <= self.cnt[s], f"wait on future {s} {v} > {self.cnt[s]}"
        if self.serialize:
            waits = tuple((s, self.cnt[s]) for s in
                          ("ld", "gld", "sp", "st", "cc", "pe", "ve", "se",
                           "lxa", "lxb", "lwa", "lwb", "spa", "spb",
                           "lv0", "lv1", "lv2", "lv3", "sta", "stb"))
        if inc is not None:
            self.cnt[inc] += n
        self.steps[eng].append((fn, tuple(waits), inc, n))

    def tok(self, s):
        return (s, self.cnt[s])

    def flush(self):
        nc = self.nc
        with nc.Block() as block:
            for e in ENGS:
                items = self.steps[e]
                self.steps[e] = []
                if not items:
                    continue
                lw = self.last_wait[e]

                def body(eng, items=items, lw=lw, e_name=e):
                    for fn, waits, inc, n in items:
                        for s, v in waits:
                            if lw.get(s, -1) >= v:
                                continue
                            eng.wait_ge(self.semh[s], v)
                            lw[s] = v
                        ins = fn(eng)
                        if inc is not None:
                            if (e_name == "vector" and ins is not None
                                    and P.VDRAIN):
                                ins = eng.drain()
                            if ins is None:
                                ins = eng.nop()
                            ins.then_inc(self.semh[inc], n)

                getattr(block, e)(body)

    def barrier(self):
        c = self.cnt
        self.ev("sync", lambda e: None,
                waits=tuple((s, c[s]) for s in
                            ("ld", "st", "lxa", "lxb", "lwa", "lwb",
                             "lv0", "lv1", "lv2", "lv3", "sta", "stb",
                             "rc0", "rc1")))
        self.ev("gpsimd", lambda e: None,
                waits=(("gld", c["gld"]), ("sp", c["sp"]), ("cc", c["cc"]),
                       ("spa", c["spa"]), ("spb", c["spb"])))
        for e in ENGS:
            self.ev(e, lambda en: en.nop(), inc="ph")
        self.nphase += 1
        tgt = 5 * self.nphase
        assert self.cnt["ph"] == tgt
        for e in ENGS:
            self.ev(e, lambda en: None, waits=(("ph", tgt),))


def build():
    nc = bass.Bass("TRN2", target_bir_lowering=False)

    x_in = nc.dram_tensor("x", [T, D], F32, kind="ExternalInput")
    wq_in = nc.dram_tensor("wq", [D, D], F32, kind="ExternalInput")
    wk_in = nc.dram_tensor("wk", [D, D], F32, kind="ExternalInput")
    wv_in = nc.dram_tensor("wv", [D, D], F32, kind="ExternalInput")
    wo_in = nc.dram_tensor("wo", [D, D], F32, kind="ExternalInput")
    wg_in = nc.dram_tensor("wg", [II, D], F32, kind="ExternalInput")
    wu_in = nc.dram_tensor("wu", [II, D], F32, kind="ExternalInput")
    wd_in = nc.dram_tensor("wd", [D, II], F32, kind="ExternalInput")
    clips_in = nc.dram_tensor("clips", [1, 2], F32, kind="ExternalInput")
    identf_in = nc.dram_tensor("identf", [128, 128], F32, kind="ExternalInput")
    identb_in = nc.dram_tensor("identb", [128, 128], BF16, kind="ExternalInput")
    onesf_in = nc.dram_tensor("onesf", [128, 128], F32, kind="ExternalInput")
    onesb_in = nc.dram_tensor("onesb", [128, 8], BF16, kind="ExternalInput")
    hsel_in = nc.dram_tensor("hsel", [128, 128], BF16, kind="ExternalInput")
    maskb_in = nc.dram_tensor("maskb", [128, 128], BF16, kind="ExternalInput")
    fconst_in = nc.dram_tensor("fconst", [128, 8], F32, kind="ExternalInput")
    out = nc.dram_tensor("out", [T, D], F32, kind="ExternalOutput")

    x1d = nc.dram_tensor("x1d", [NDC, 128, T], F32)
    vd = nc.dram_tensor("vd", [NIC, 128, T], F32)
    bounce = nc.dram_tensor("bounce", [1, 64], F32)
    rc_dram = nc.dram_tensor("rc_dram", [2, 2, 1024], F32)
    cc1_in = nc.dram_tensor("cc1_in", [1, 16], F32)
    cc1_out = nc.dram_tensor("cc1_out", [1, 128], F32)
    cc2_in = nc.dram_tensor("cc2_in", [1, 16], F32)
    cc2_out = nc.dram_tensor("cc2_out", [1, 128], F32)

    KDBG = os.environ.get("KDBG", "")
    KSER = set(x for x in os.environ.get("KSER", "").split(",") if x)
    dbg = {}
    if "x1" in KDBG:
        dbg["x1"] = nc.dram_tensor("dbg_x1", [NSEQ, 128, 16384], F32,
                                   kind="ExternalOutput")
    if "qkv" in KDBG:
        dbg["qt"] = nc.dram_tensor("dbg_qt", [NSEQ, 128, 16384], F32,
                                   kind="ExternalOutput")
        dbg["kt"] = nc.dram_tensor("dbg_kt", [NSEQ, 128, 16384], F32,
                                   kind="ExternalOutput")
        dbg["vh"] = nc.dram_tensor("dbg_vh", [NSEQ, 128, 16640], F32,
                                   kind="ExternalOutput")
        dbg["negc"] = nc.dram_tensor("dbg_negc", [NSEQ, 128, 32], F32,
                                     kind="ExternalOutput")
    if "negc2" in KDBG:
        dbg["qnorm"] = nc.dram_tensor("dbg_qnorm", [NSEQ, 16, 2048], F32,
                                      kind="ExternalOutput")
        dbg["ssc"] = nc.dram_tensor("dbg_ssc", [NSEQ, 128, 32], F32,
                                    kind="ExternalOutput")
        dbg["sflat"] = nc.dram_tensor("dbg_sflat", [NSEQ, 1, 64], F32,
                                      kind="ExternalOutput")
        dbg["srow32"] = nc.dram_tensor("dbg_srow32", [NSEQ, 1, 32], F32,
                                       kind="ExternalOutput")
    if "ot" in KDBG:
        dbg["ot"] = nc.dram_tensor("dbg_ot", [NSEQ, 128, 16384], F32,
                                   kind="ExternalOutput")
    if "ffn" in KDBG:
        dbg["xint"] = nc.dram_tensor("dbg_xint", [128, 32768], F32,
                                     kind="ExternalOutput")
        dbg["scales"] = nc.dram_tensor("dbg_scales", [128, 96], F32,
                                       kind="ExternalOutput")

    p = P(nc)
    for s in ("ld", "gld", "sp", "st", "cc", "pe", "ve", "se", "ge", "ph",
              "lxa", "lxb", "lwa", "lwb", "spa", "spb",
              "lv0", "lv1", "lv2", "lv3", "sta", "stb", "rc0", "rc1"):
        p.sem(s)
    ev, tok, cnt = p.ev, p.tok, p.cnt

    per = ExitStack()

    _uniq = [0]

    def sbt(name, shape, dt=F32, stack=None, side=None):
        kw = {"side": side} if side else {}
        _uniq[0] += 1
        return (stack or per).enter_context(
            nc.sbuf_tensor(f"{name}__{_uniq[0]}", shape, dt, **kw))

    def pst(name, shape, dt=F32, stack=None):
        _uniq[0] += 1
        return (stack or per).enter_context(
            nc.psum_tensor(f"{name}__{_uniq[0]}", shape, dt))

    identf = sbt("s_identf", [128, 128])
    identb = sbt("s_identb", [128, 128], BF16)
    onesf = sbt("s_onesf", [128, 128])
    onesb = sbt("s_onesb", [128, 8], BF16)
    hsel = sbt("s_hsel", [128, 128], BF16)
    maskb = sbt("s_maskb", [128, 128], BF16)
    ssc = sbt("ssc", [128, 32])
    amax_buf = sbt("amax_buf", [128, 64])
    vmax_buf = sbt("vmax_buf", [128, 24])
    wsu_sc = sbt("wsu_sc", [128, 24])
    qscale_g = sbt("qscale_g", [128, 24])
    scd_g = sbt("scd_g", [128, 8])
    xinvb = sbt("xinvb", [128, 2])
    hinvb = sbt("hinvb", [128, 2])
    clips_sb = sbt("clips_sb", [1, 2])
    fc = sbt("s_fconst", [128, 8])
    srow = sbt("srow", [128, 2048])
    sflat = sbt("sflat", [1, 64])

    for dst, src in ((identf, identf_in), (identb, identb_in), (onesf, onesf_in),
                     (onesb, onesb_in), (hsel, hsel_in), (maskb, maskb_in),
                     (clips_sb, clips_in), (fc, fconst_in)):
        ev("sync", lambda e, d=dst, s=src: e.dma_start(out=d[:, :], in_=s[:, :]),
           inc="ld", n=16)
    ev("vector", lambda e: e.memset(vmax_buf[:, :], 0.0), inc="ve")
    ev("vector", lambda e: e.memset(amax_buf[:, :], 0.0), inc="ve")
    p.barrier()
    p.flush()

    seq_es = ExitStack()

    def attention_seq(s):
        ph = ExitStack()
        att_es = ExitStack()
        Qt = sbt("Qt", [128, 16384], BF16, att_es)
        Kt = sbt("Kt", [128, 16384], BF16, att_es)
        Vhat = sbt("Vhat", [128, 16640], BF16, att_es)
        negc_b = sbt("negc_b", [128, 32], F32, att_es)
        xTbf = sbt("xTbf", [128, 16384], BF16, ph)
        wqb = sbt("wqb", [128, 8192], BF16, ph)
        wkb = sbt("wkb", [128, 8192], BF16, ph)
        wvb = sbt("wvb", [128, 8192], BF16, ph)
        xsb = [sbt(f"xsb_{i}", [128, 1024], BF16, ph) for i in range(2)]
        xs32p = [sbt("xs32p_0", [128, 1024], F32, ph)] * 2
        qsq = [sbt(f"qsq_{i}", [128, 512], BF16, ph) for i in range(2)]
        qnorm = sbt("qnorm", [16, 2048], F32, ph)
        ktmp = sbt("ktmp", [16, 512], F32, ph)
        ptrb = [pst(f"ptrb_{i}", [128, 1024], BF16, ph) for i in range(2)]
        psq = [pst(f"psq_{i}", [128, 512], F32, ph) for i in range(3)]
        pnorm = pst("pnorm", [16, 512], F32, ph)
        pbc = pst("pbc", [128, 512], F32, ph)

        p.serialize = "p1" in KSER
        for wsb, wdr in ((wqb, wq_in), (wkb, wk_in), (wvb, wv_in)):
            for dc in range(NDC):
                ev("gpsimd", lambda e, d=wsb, ss=wdr, dc=dc:
                   e.dma_start(out=d[:, dc * 1024:(dc + 1) * 1024],
                               in_=ss[dc * 128:(dc + 1) * 128, :]),
                   inc="gld", n=16)
        w_tok = tok("gld")

        p.serialize = ("p1" in KSER) or ("p1a" in KSER)
        KVAR = os.environ.get("KVAR", "")
        cast_t = {}
        evac_t = {}
        for tc in range(16):
            bi = tc % 2
            if "synccast" in KVAR:
                waits = []
                if tc >= 1:
                    waits.append(cast_t[tc - 1])
                lxs = "lxa" if tc % 2 == 0 else "lxb"
                ev("sync", lambda e, bi=bi, tc=tc:
                   e.dma_start(out=xs32p[bi][:, :],
                               in_=x_in[s * L + tc * 128: s * L + (tc + 1) * 128, :]),
                   waits=waits, inc=lxs, n=16)
                ldf = tok(lxs)
                wc = [ldf]
                if tc >= 2:
                    wc.append(evac_t[tc - 2])
                ev("vector", lambda e, bi=bi: e.tensor_copy(xsb[bi][:, :],
                                                            xs32p[bi][:, :]),
                   waits=wc, inc="ve")
                cast_t[tc] = tok("ve")
                ld_t = cast_t[tc]
            else:
                waits = []
                if tc >= 2:
                    waits.append(evac_t[tc - 2])
                lxs = "lxa" if tc % 2 == 0 else "lxb"
                ev("gpsimd", lambda e, bi=bi, tc=tc:
                   e.dma_start(out=xsb[bi][:, :],
                               in_=x_in[s * L + tc * 128: s * L + (tc + 1) * 128, :]),
                   waits=waits, inc=lxs, n=16)
                ld_t = tok(lxs)
            wv_ = [ld_t]
            if tc >= 2:
                wv_.append(evac_t[tc - 2])
            for j in range(8):
                ev("tensor", lambda e, bi=bi, j=j:
                   e.transpose(ptrb[bi][:, j * 128:(j + 1) * 128],
                               xsb[bi][:, j * 128:(j + 1) * 128], identb[:, :]),
                   waits=(wv_ if j == 0 else ()))
            ev("tensor", lambda e: e.drain(), inc="pe")
            tr_t = tok("pe")
            ev("vector", lambda e, bi=bi, tc=tc: e.tensor_copy(
                xTbf[:, :].rearrange("p (c t) -> p c t", c=8)[:, :, tc * 128:(tc + 1) * 128],
                ptrb[bi][:, :].rearrange("p (c t) -> p c t", c=8)),
               waits=[tr_t], inc="ve")
            evac_t[tc] = tok("ve")
        xT_done = tok("ve")

        p.serialize = ("p1" in KSER) or ("p1b" in KSER)
        pq_free = [None, None, None]

        def proj_qk(wsb, dst):
            nonlocal pq_free
            for hc in range(NHC):
                for tt in range(4):
                    k = (hc * 4 + tt) % 3
                    waits = [xT_done, w_tok]
                    if pq_free[k] is not None:
                        waits.append(pq_free[k])
                    for dc in range(NDC):
                        ev("tensor", lambda e, k=k, dc=dc, hc=hc, tt=tt, wsb=wsb:
                           e.matmul(psq[k][:, :],
                                    wsb[:, dc * 1024 + hc * 128: dc * 1024 + (hc + 1) * 128],
                                    xTbf[:, dc * 2048 + tt * 512: dc * 2048 + (tt + 1) * 512],
                                    start=(dc == 0), stop=(dc == NDC - 1)),
                           waits=(waits if dc == 0 else ()),
                           inc=("pe" if dc == NDC - 1 else None))
                    mm_t = tok("pe")
                    if dst is Qt:
                        ev("vector", lambda e, k=k, hc=hc, tt=tt, dst=dst:
                           e.tensor_scalar(dst[:, hc * 2048 + tt * 512: hc * 2048 + (tt + 1) * 512],
                                           psq[k][:, :], 0.125, None, Alu.mult),
                           waits=[mm_t], inc="ve")
                    else:
                        ev("vector", lambda e, k=k, hc=hc, tt=tt, dst=dst:
                           e.tensor_copy(dst[:, hc * 2048 + tt * 512: hc * 2048 + (tt + 1) * 512],
                                         psq[k][:, :]),
                           waits=[mm_t], inc="ve")
                    pq_free[k] = tok("ve")

        proj_qk(wqb, Qt)
        p.serialize = ("p1" in KSER) or ("p1c" in KSER)
        proj_qk(wkb, Kt)

        p.serialize = ("p1" in KSER) or ("p1d" in KSER)
        for tc in range(16):
            ev("vector", lambda e, tc=tc: e.memset(
                Vhat[:, tc * 1040:(tc + 1) * 1040]
                .rearrange("p (h e) -> p h e", h=16)[:, :, 64:65], 1.0),
               waits=([] if tc else [w_tok]), inc="ve")
            ones_t = tok("ve")
            for half in range(2):
                k = (tc * 2 + half) % 3
                waits = [xT_done, w_tok]
                if pq_free[k] is not None:
                    waits.append(pq_free[k])
                for dc in range(NDC):
                    ev("tensor", lambda e, k=k, dc=dc, tc=tc, half=half:
                       e.matmul(psq[k][:, :],
                                xTbf[:, dc * 2048 + tc * 128: dc * 2048 + (tc + 1) * 128],
                                wvb[:, dc * 1024 + half * 512: dc * 1024 + (half + 1) * 512],
                                start=(dc == 0), stop=(dc == NDC - 1)),
                       waits=(waits if dc == 0 else ()),
                       inc=("pe" if dc == NDC - 1 else None))
                mm_t = tok("pe")
                ev("vector", lambda e, k=k, tc=tc, half=half:
                   e.tensor_copy(
                       Vhat[:, tc * 1040 + half * 520: tc * 1040 + (half + 1) * 520]
                       .rearrange("p (h e) -> p h e", h=8)[:, :, 0:64],
                       psq[k][:, :].rearrange("p (h e) -> p h e", h=8)),
                   waits=[mm_t, ones_t], inc="ve")
                pq_free[k] = tok("ve")
        qkv_done_ve = tok("ve")

        p.serialize = ("p1" in KSER) or ("p1e" in KSER)
        def norms(src, dstn):
            sq_free = [None, None]
            prev_ln = None
            for tt in range(4):
                for hp in range(NHC):
                    k = hp % 2
                    waits = [qkv_done_ve]
                    if sq_free[k] is not None:
                        waits.append(sq_free[k])
                    ev("vector", lambda e, k=k, hp=hp, tt=tt, src=src:
                       e.tensor_tensor(qsq[k][:, :],
                                       src[:, hp * 2048 + tt * 512: hp * 2048 + (tt + 1) * 512],
                                       src[:, hp * 2048 + tt * 512: hp * 2048 + (tt + 1) * 512],
                                       Alu.mult),
                       waits=waits, inc="ve")
                    sq_t = tok("ve")
                    w_mm = [sq_t]
                    if hp == 0 and prev_ln is not None:
                        w_mm.append(prev_ln)
                    ev("tensor", lambda e, k=k, hp=hp:
                       e.matmul(pnorm[:, :],
                                hsel[:, hp * 16:(hp + 1) * 16], qsq[k][:, :],
                                start=(hp == 0), stop=(hp == NHC - 1)),
                       waits=w_mm, inc="pe")
                    sq_free[k] = tok("pe")
                nm_t = tok("pe")
                ev("scalar", lambda e, dstn=dstn, tt=tt: e.activation(
                    dstn[:, tt * 512:(tt + 1) * 512], pnorm[:, :], AF.Ln,
                    bias=fc[0:16, 0:1], scale=1.0),
                   waits=[nm_t], inc="se")
                ev("scalar", lambda e, dstn=dstn, tt=tt: e.activation(
                    dstn[:, tt * 512:(tt + 1) * 512],
                    dstn[:, tt * 512:(tt + 1) * 512], AF.Exp,
                    bias=0.0, scale=fc[0:16, 2:3]), inc="se")
                prev_ln = tok("se")
            return prev_ln

        qn_t = norms(Qt, qnorm)

        sq_free = [None, None]
        prev_ln = qn_t
        for tt in range(4):
            for hp in range(NHC):
                k = hp % 2
                waits = [qkv_done_ve]
                if sq_free[k] is not None:
                    waits.append(sq_free[k])
                ev("vector", lambda e, k=k, hp=hp, tt=tt:
                   e.tensor_tensor(qsq[k][:, :],
                                   Kt[:, hp * 2048 + tt * 512: hp * 2048 + (tt + 1) * 512],
                                   Kt[:, hp * 2048 + tt * 512: hp * 2048 + (tt + 1) * 512],
                                   Alu.mult),
                   waits=waits, inc="ve")
                sq_t = tok("ve")
                w_mm = [sq_t]
                if hp == 0:
                    w_mm.append(prev_ln)
                ev("tensor", lambda e, k=k, hp=hp:
                   e.matmul(pnorm[:, :], hsel[:, hp * 16:(hp + 1) * 16], qsq[k][:, :],
                            start=(hp == 0), stop=(hp == NHC - 1)),
                   waits=w_mm, inc="pe")
                sq_free[k] = tok("pe")
            nm_t = tok("pe")
            kred_prev = tok("ve")
            ev("scalar", lambda e, tt=tt: e.activation(
                ktmp[:, :], pnorm[:, :], AF.Ln, bias=fc[0:16, 0:1], scale=1.0),
               waits=[nm_t, kred_prev], inc="se")
            ev("scalar", lambda e: e.activation(
                ktmp[:, :], ktmp[:, :], AF.Exp, bias=0.0, scale=fc[0:16, 2:3]), inc="se")
            prev_ln = tok("se")
            ev("vector", lambda e, tt=tt: e.tensor_reduce(
                ssc[0:16, 16 + tt:17 + tt], ktmp[:, :], axis=AX, op=Alu.max),
               waits=[prev_ln], inc="ve")
        kr_t = tok("ve")
        ev("vector", lambda e: e.tensor_reduce(ssc[0:16, 0:1], ssc[0:16, 16:20],
                                               axis=AX, op=Alu.max),
           waits=[kr_t], inc="ve")
        _t = tok("ve")
        ev("vector", lambda e: e.tensor_scalar(ssc[0:16, 1:2], ssc[0:16, 0:1],
                                               -1.0, None, Alu.mult),
           waits=[_t], inc="ve")
        kx_t = tok("ve")
        ev("vector", lambda e: e.tensor_scalar(qnorm[:, :], qnorm[:, :],
                                               ssc[0:16, 1:2], None, Alu.mult),
           waits=[qn_t, kx_t], inc="ve")
        for qt in range(4):
            ev("vector", lambda e, qt=qt: e.tensor_reduce(
                ssc[0:16, 2 + qt:3 + qt], qnorm[:, qt * 512:(qt + 1) * 512],
                axis=AX, op=Alu.min), inc="ve")
        red_t = tok("ve")
        ev("sync", lambda e: e.dma_start(
            out=bounce.ap().rearrange("o (a b) -> (o a) b", a=16),
            in_=ssc[0:16, 2:6]), waits=[red_t], inc="ld", n=16)
        b1 = tok("ld")
        ev("sync", lambda e: e.dma_start(out=sflat[:, :], in_=bounce[:, :]),
           waits=[b1], inc="ld", n=16)
        b2 = tok("ld")
        ev("vector", lambda e: e.tensor_tensor(
            srow[0:1, 0:32].rearrange("o (p q) -> o p q", p=8),
            sflat[0:1, :].rearrange("o (p q) -> o p q", p=8)[:, :, 0:4],
            sflat[0:1, :].rearrange("o (p q) -> o p q", p=8)[:, :, 4:8],
            Alu.min), waits=[b2], inc="ve")
        pm_t = tok("ve")
        ev("tensor", lambda e: e.matmul(pbc[:, 0:32], onesf[0:1, :], srow[0:1, 0:32],
                                        start=True, stop=True),
           waits=[pm_t], inc="pe")
        bc_t = tok("pe")
        ev("vector", lambda e: e.tensor_copy(negc_b[:, :], pbc[:, 0:32]),
           waits=[bc_t], inc="ve")

        if "qnorm" in dbg:
            ev("sync", lambda e: e.dma_start(out=dbg["qnorm"][s], in_=qnorm[:, :]),
               waits=[tok("ve")], inc="ld", n=16)
            ev("sync", lambda e: e.dma_start(out=dbg["ssc"][s], in_=ssc[:, :]),
               inc="ld", n=16)
            ev("sync", lambda e: e.dma_start(out=dbg["sflat"][s], in_=sflat[:, :]),
               inc="ld", n=16)
            ev("sync", lambda e: e.dma_start(out=dbg["srow32"][s], in_=srow[0:1, 0:32]),
               inc="ld", n=16)
        if "qt" in dbg:
            ev("gpsimd", lambda e: e.dma_start(out=dbg["qt"][s], in_=Qt[:, :]),
               waits=[tok("ve")], inc="gld", n=16)
            ev("gpsimd", lambda e: e.dma_start(out=dbg["kt"][s], in_=Kt[:, :]),
               inc="gld", n=16)
            ev("gpsimd", lambda e: e.dma_start(out=dbg["vh"][s], in_=Vhat[:, :]),
               inc="gld", n=16)
            ev("gpsimd", lambda e: e.dma_start(out=dbg["negc"][s], in_=negc_b[:, :]),
               inc="gld", n=16)

        p.barrier()
        p.flush()
        ph.close()

        ph = ExitStack()
        OT = sbt("OT", [128, 16384], BF16, seq_es, side="right")
        ab = [sbt(f"ab_{i}", [128, 1024], BF16, ph) for i in range(4)]
        osb = [sbt(f"osb_{i}", [128, 1024], BF16, ph) for i in range(2)]
        rcb = [sbt(f"rcb_{i}", [64, 1024], F32, ph) for i in range(2)]
        erow = sbt("erow", [128, 4096], F32, ph)
        psS = [pst(f"psS_{i}", [128, 1024], F32, ph) for i in range(2)]
        psO = [pst(f"psO_{i}", [128, 1024], F32, ph) for i in range(2)]

        p.serialize = "p2" in KSER
        exp_done = {}
        av_done = {}
        psO_free = {}
        pending = []

        tt_done = {}

        def emit_pending():
            for ctx, hp0, qt0 in pending:
                rcs = f"rc{ctx}"
                w_tt = [(rcs, cnt[rcs])]
                for h in range(2):
                    ev("vector", lambda e, h=h, hp0=hp0, qt0=qt0, ctx=ctx:
                       e.tensor_tensor(
                        OT[h * 64:(h + 1) * 64,
                           hp0 * 2048 + qt0 * 512: hp0 * 2048 + (qt0 + 1) * 512],
                        osb[ctx][0:64, h * 512:(h + 1) * 512],
                        rcb[ctx][:, h * 512:(h + 1) * 512], Alu.mult),
                       waits=(w_tt if h == 0 else ()), inc="ve")
                tt_done[ctx] = tok("ve")
            pending.clear()

        for hpp in range(4):
            for qt in range(4):
                nkc = 4 * qt + 4
                av_args = {}

                def emit_av(ctx, kc):
                    (ai, hp, w_av, col0, span) = av_args.pop(ctx)
                    for h in range(2):
                        ev("tensor", lambda e, ctx=ctx, ai=ai, hp=hp, kc=kc, h=h,
                           col0=col0, span=span, kcf=(kc == 0):
                           e.matmul(psO[ctx][0:65, h * 512 + col0: h * 512 + col0 + span],
                                    Vhat[:, kc * 1040 + (2 * hp + h) * 65: kc * 1040 + (2 * hp + h) * 65 + 65],
                                    ab[ai][:, h * 512: h * 512 + span],
                                    start=kcf, stop=(kc == nkc - 1)),
                           waits=(w_av if h == 0 else ()),
                           inc=("pe" if h == 1 else None))
                    av_done[(ctx, kc % 2)] = tok("pe")

                for kc in range(nkc):
                    if kc == 2:
                        emit_pending()
                    qs = max(qt * 512, kc * 128)
                    span = (qt + 1) * 512 - qs
                    diag = kc * 128 >= qt * 512
                    s_toks = {}
                    for ctx in range(2):
                        hp = 2 * hpp + ctx
                        w_s = []
                        if ctx in exp_done:
                            w_s.append(exp_done[ctx])
                        ev("tensor", lambda e, ctx=ctx, hp=hp, kc=kc, qs=qs, span=span:
                           e.matmul(psS[ctx][:, 0:span],
                                    Kt[0:64, hp * 2048 + kc * 128: hp * 2048 + (kc + 1) * 128],
                                    Qt[0:64, hp * 2048 + qs: hp * 2048 + qs + span],
                                    start=True, stop=True, tile_position=(0, 0)),
                           waits=w_s, inc=None)
                        ev("tensor", lambda e, ctx=ctx, hp=hp, kc=kc, qs=qs, span=span:
                           e.matmul(psS[ctx][:, 512:512 + span],
                                    Kt[64:128, hp * 2048 + kc * 128: hp * 2048 + (kc + 1) * 128],
                                    Qt[64:128, hp * 2048 + qs: hp * 2048 + qs + span],
                                    start=True, stop=True, tile_position=(64, 0)),
                           inc="pe")
                        s_toks[ctx] = tok("pe")
                    for ctx in range(2):
                        if ctx in av_args:
                            emit_av(ctx, kc - 1)
                    for ctx in range(2):
                        hp = 2 * hpp + ctx
                        ai = ctx * 2 + kc % 2
                        w_e = [s_toks[ctx]]
                        if (ctx, kc % 2) in av_done:
                            w_e.append(av_done[(ctx, kc % 2)])
                        ev("scalar", lambda e, ctx=ctx, ai=ai, hp=hp, qt=qt, span=span:
                           e.activation(
                               ab[ai][:, :].rearrange("p (h q) -> p h q", h=2)[:, :, 0:span],
                               psS[ctx][:, :].rearrange("p (h q) -> p h q", h=2)[:, :, 0:span],
                               AF.Exp, bias=negc_b[:, hp * 4 + qt: hp * 4 + qt + 1],
                               scale=1.0),
                           waits=w_e, inc="se")
                        exp_done[ctx] = tok("se")
                        w_av = [exp_done[ctx]]
                        if diag:
                            for h in range(2):
                                ev("vector", lambda e, ai=ai, h=h:
                                   e.tensor_tensor(ab[ai][:, h * 512: h * 512 + 128],
                                                   ab[ai][:, h * 512: h * 512 + 128],
                                                   maskb[:, :], Alu.mult),
                                   waits=([exp_done[ctx]] if h == 0 else ()), inc="ve")
                            w_av.append(tok("ve"))
                        if kc == 0 and ctx in psO_free:
                            w_av.extend(psO_free[ctx])
                        col0 = qs - qt * 512
                        av_args[ctx] = (ai, 2 * hpp + ctx, w_av, col0, span)
                for ctx in range(2):
                    emit_av(ctx, nkc - 1)
                for ctx in range(2):
                    hp = 2 * hpp + ctx
                    rcs = f"rc{ctx}"
                    last_av = av_done[(ctx, (nkc - 1) % 2)]
                    ev("vector", lambda e, ctx=ctx: e.tensor_copy(
                        osb[ctx][0:64, :], psO[ctx][0:64, 0:1024]),
                       waits=[last_av], inc="ve")
                    ev("vector", lambda e, ctx=ctx: e.tensor_copy(
                        erow[64:65, ctx * 1024:(ctx + 1) * 1024],
                        psO[ctx][64:65, 0:1024]),
                       waits=[(rcs, cnt[rcs])], inc="ve")
                    ocp_t = tok("ve")
                    psO_free[ctx] = (ocp_t,)
                    ev("sync", lambda e, ctx=ctx: e.dma_start(
                        out=rc_dram[ctx, 0], in_=erow[64:65, ctx * 1024:(ctx + 1) * 1024]),
                       waits=[ocp_t], inc=rcs, n=16)
                    ev("sync", lambda e, ctx=ctx: e.dma_start(
                        out=erow[0:1, ctx * 1024:(ctx + 1) * 1024],
                        in_=rc_dram[ctx, 0][None, :] if False else rc_dram[ctx, 0:1, :].rearrange("a b -> a b")),
                       waits=[(rcs, cnt[rcs])], inc=rcs, n=16)
                    ld2 = (rcs, cnt[rcs])
                    ev("scalar", lambda e, ctx=ctx: e.activation(
                        erow[0:1, 2048 + ctx * 1024: 2048 + (ctx + 1) * 1024],
                        erow[0:1, ctx * 1024:(ctx + 1) * 1024], AF.Ln,
                        bias=0.0, scale=1.0),
                       waits=[ld2], inc="se")
                    ev("scalar", lambda e, ctx=ctx: e.activation(
                        erow[0:1, 2048 + ctx * 1024: 2048 + (ctx + 1) * 1024],
                        erow[0:1, 2048 + ctx * 1024: 2048 + (ctx + 1) * 1024], AF.Exp,
                        bias=0.0, scale=fc[0:1, 4:5]), inc="se")
                    rcp_t = tok("se")
                    ev("sync", lambda e, ctx=ctx: e.dma_start(
                        out=rc_dram[ctx, 1], in_=erow[0:1, 2048 + ctx * 1024: 2048 + (ctx + 1) * 1024]),
                       waits=[rcp_t], inc=rcs, n=16)
                    w_bc = [(rcs, cnt[rcs])]
                    if ctx in tt_done:
                        w_bc.append(tt_done[ctx])
                    ev("sync", lambda e, ctx=ctx: e.dma_start(
                        out=rcb[ctx][:, :],
                        in_=bass.AP(rc_dram, ctx * 2048 + 1024, [[0, 64], [1, 1024]])),
                       waits=w_bc, inc=rcs, n=16)
                    pending.append((ctx, hp, qt))
        emit_pending()

        if "ot" in dbg:
            ev("gpsimd", lambda e: e.dma_start(out=dbg["ot"][s], in_=OT[:, :]),
               waits=[tok("ve")], inc="gld", n=16)

        p.barrier()
        p.flush()
        ph.close()
        att_es.close()

        ph = ExitStack()
        xT32 = sbt("xT32", [128, 16384], F32, seq_es, side="right")
        wob = sbt("wob", [128, 8192], BF16, ph)
        rb = sbt("rb", [128, 4096], BF16, ph)
        rsq = [sbt(f"rsq_{i}", [128, 512], BF16, ph) for i in range(3)]
        xs32b = [sbt(f"xs32b_{i}", [128, 512], F32, ph) for i in range(2)]
        pr = [pst(f"pr_{i}", [128, 512], F32, ph) for i in range(3)]
        ptr32 = pst("ptr32", [128, 512], F32, ph)
        pm = pst("pm", [1, 512], F32, ph)
        ps2 = pst("ps2", [1, 512], F32, ph)
        pbc_nm = pst("pbc_nm", [128, 512], F32, ph)
        pbc_rs = pst("pbc_rs", [128, 512], F32, ph)

        for hp in range(NHC):
            ev("gpsimd", lambda e, hp=hp: e.dma_start(
                out=wob[:, hp * 1024:(hp + 1) * 1024],
                in_=wo_in[hp * 128:(hp + 1) * 128, :]), inc="gld", n=16)
        wo_tok = tok("gld")

        p.serialize = "p3" in KSER
        xr_state = {}

        def emit_xreload(tc):
            for g in range(2):
                idx = 2 * tc + g
                bi = idx % 2
                waits = []
                if ("evac", idx - 2) in xr_state:
                    waits.append(xr_state[("evac", idx - 2)])
                lxs = "lxa" if idx % 2 == 0 else "lxb"
                ev("sync", lambda e, bi=bi, tc=tc, g=g:
                   e.dma_start(out=xs32b[bi][:, :],
                               in_=x_in[s * L + tc * 128: s * L + (tc + 1) * 128,
                                        g * 512:(g + 1) * 512]),
                   waits=waits, inc=lxs, n=16)
                ld_t = tok(lxs)
                wv_ = [ld_t]
                if ("evac", idx - 1) in xr_state:
                    wv_.append(xr_state[("evac", idx - 1)])
                for j in range(4):
                    ev("tensor", lambda e, bi=bi, j=j:
                       e.transpose(ptr32[:, j * 128:(j + 1) * 128],
                                   xs32b[bi][:, j * 128:(j + 1) * 128], identf[:, :]),
                       waits=(wv_ if j == 0 else ()))
                ev("tensor", lambda e: e.drain(), inc="pe")
                tr_t = tok("pe")
                ev("vector", lambda e, tc=tc, g=g: e.tensor_copy(
                    xT32[:, :].rearrange("p (c t) -> p c t", c=8)[:, g * 4:(g + 1) * 4,
                                                                 tc * 128:(tc + 1) * 128],
                    ptr32[:, :].rearrange("p (j t) -> p j t", j=4)),
                   waits=[tr_t], inc="ve")
                xr_state[("evac", idx)] = tok("ve")

        pr_free = [None, None, None]
        bc_free = None
        stats_pend = []

        def flush_stats(bcf):
            for (dcx, ct2, sqt) in stats_pend:
                w_m1 = [ct2]
                if dcx == 0 and bcf is not None:
                    w_m1.append(bcf)
                ev("tensor", lambda e, dcx=dcx: e.matmul(
                    pm[:, :], onesb[:, 0:1], rb[:, dcx * 512:(dcx + 1) * 512],
                    start=(dcx == 0), stop=(dcx == NDC - 1)), waits=w_m1, inc=None)
                ev("tensor", lambda e, dcx=dcx: e.matmul(
                    ps2[:, :], onesb[:, 0:1], rsq[dcx % 3][:, :],
                    start=(dcx == 0), stop=(dcx == NDC - 1)), waits=[sqt], inc="pe")
            stats_pend.clear()

        for tt in range(4):
            p.serialize = ("p3" in KSER) or ("p3a" in KSER)
            for tc in range(4 * tt, 4 * tt + 4):
                emit_xreload(tc)
            p.serialize = ("p3" in KSER) or ("p3b" in KSER)
            xr_t = tok("ve")
            r_toks = []
            for dc in range(NDC):
                k = (tt * NDC + dc) % 3
                waits = [wo_tok]
                if pr_free[k] is not None:
                    waits.append(pr_free[k])
                for hp in range(NHC):
                    ev("tensor", lambda e, k=k, hp=hp, dc=dc, tt=tt:
                       e.matmul(pr[k][:, :],
                                wob[:, hp * 1024 + dc * 128: hp * 1024 + (dc + 1) * 128],
                                OT[:, hp * 2048 + tt * 512: hp * 2048 + (tt + 1) * 512],
                                start=(hp == 0), stop=(hp == NHC - 1)),
                       waits=(waits if hp == 0 else ()),
                       inc=("pe" if hp == NHC - 1 else None))
                mm_t = tok("pe")
                flush_stats(bc_free)
                ev("vector", lambda e, k=k, dc=dc, tt=tt, _w=xr_t: e.scalar_tensor_tensor(
                    xT32[:, dc * 2048 + tt * 512: dc * 2048 + (tt + 1) * 512],
                    pr[k][:, :], 1.0,
                    xT32[:, dc * 2048 + tt * 512: dc * 2048 + (tt + 1) * 512],
                    Alu.mult, Alu.add),
                   waits=[mm_t, xr_t], inc="ve")
                pr_free[k] = tok("ve")
                r_toks.append(tok("ve"))
                r_t = tok("ve")
                ev("scalar", lambda e, dc=dc, tt=tt: e.copy(
                    rb[:, dc * 512:(dc + 1) * 512],
                    xT32[:, dc * 2048 + tt * 512: dc * 2048 + (tt + 1) * 512]),
                   waits=[r_t], inc="se")
                cast_t2 = tok("se")
                ev("gpsimd", lambda e, dc=dc: e.tensor_tensor(
                    rsq[dc % 3][:, :], rb[:, dc * 512:(dc + 1) * 512],
                    rb[:, dc * 512:(dc + 1) * 512], Alu.mult),
                   waits=([cast_t2] if dc < 3 else [cast_t2, tok("pe")]), inc="ge")
                sq_t = tok("ge")
                stats_pend.append((dc, cast_t2, sq_t))
            flush_stats(bc_free)
            st_mm = tok("pe")
            ev("vector", lambda e: e.tensor_scalar(srow[0:1, 0:512], pm[:, :],
                                                   1.0 / D, None, Alu.mult),
               waits=[st_mm], inc="ve")
            ev("vector", lambda e: e.tensor_tensor(srow[0:1, 512:1024],
                                                   srow[0:1, 0:512], srow[0:1, 0:512],
                                                   Alu.mult), inc="ve")
            ev("vector", lambda e: e.scalar_tensor_tensor(
                srow[0:1, 1024:1536], ps2[:, :], 1.0 / D, srow[0:1, 512:1024],
                Alu.mult, Alu.subtract), inc="ve")
            ev("vector", lambda e: e.tensor_scalar(srow[0:1, 1536:2048],
                                                   srow[0:1, 0:512], -1.0, None,
                                                   Alu.mult), inc="ve")
            var_t = tok("ve")
            ev("scalar", lambda e: e.activation(srow[0:1, 1024:1536],
                                                srow[0:1, 1024:1536], AF.Ln,
                                                bias=fc[0:1, 1:2], scale=1.0),
               waits=[var_t], inc="se")
            ev("scalar", lambda e: e.activation(srow[0:1, 1024:1536],
                                                srow[0:1, 1024:1536], AF.Exp,
                                                bias=0.0, scale=fc[0:1, 3:4]), inc="se")
            rstd_t = tok("se")
            ev("tensor", lambda e: e.matmul(pbc_nm[:, :], onesf[0:1, :],
                                            srow[0:1, 1536:2048], start=True, stop=True),
               waits=[rstd_t], inc="pe")
            ev("tensor", lambda e: e.matmul(pbc_rs[:, :], onesf[0:1, :],
                                            srow[0:1, 1024:1536], start=True, stop=True),
               inc="pe")
            bct = tok("pe")
            for dc in range(NDC):
                ev("vector", lambda e, dc=dc, tt=tt: e.tensor_tensor(
                    xT32[:, dc * 2048 + tt * 512: dc * 2048 + (tt + 1) * 512],
                    xT32[:, dc * 2048 + tt * 512: dc * 2048 + (tt + 1) * 512],
                    pbc_nm[:, :], Alu.add),
                   waits=[bct] if dc == 0 else (), inc=None)
                ev("vector", lambda e, dc=dc, tt=tt: e.tensor_tensor(
                    xT32[:, dc * 2048 + tt * 512: dc * 2048 + (tt + 1) * 512],
                    xT32[:, dc * 2048 + tt * 512: dc * 2048 + (tt + 1) * 512],
                    pbc_rs[:, :], Alu.mult), inc="ve")
                x1_t = tok("ve")
                ev("vector", lambda e, dc=dc, tt=tt, s=s: e.tensor_reduce(
                    amax_buf[:, s * 32 + tt * 8 + dc: s * 32 + tt * 8 + dc + 1],
                    xT32[:, dc * 2048 + tt * 512: dc * 2048 + (tt + 1) * 512],
                    axis=AX, op=Alu.max, apply_absolute_value=True), inc="ve")
                ev("sync", lambda e, dc=dc, tt=tt, s=s: e.dma_start(
                    out=x1d[dc, :, s * L + tt * 512: s * L + (tt + 1) * 512],
                    in_=xT32[:, dc * 2048 + tt * 512: dc * 2048 + (tt + 1) * 512]),
                   waits=[x1_t], inc="ld", n=16)
            bc_free = tok("ve")

        if "x1" in dbg:
            ev("sync", lambda e: e.dma_start(out=dbg["x1"][s], in_=xT32[:, :]),
               waits=[tok("ve")], inc="ld", n=16)

        p.barrier()
        p.flush()
        ph.close()
        seq_es.close()

    for s in range(NSEQ):
        attention_seq(s)

    ph = ExitStack()
    pfold = pst("pfold", [128, 512], F32, ph)

    def fold_and_scales(buf, ncols, cc_i, cc_o, sagc, clip_col, invb):
        ev("vector", lambda e: e.tensor_reduce(ssc[:, 8:9], buf[:, 0:ncols], axis=AX,
                                               op=Alu.max, apply_absolute_value=True),
           inc="ve")
        f1 = tok("ve")
        ev("tensor", lambda e: e.transpose(pfold[0:1, 0:128], ssc[:, 8:9], identf[:, :]),
           waits=[f1])
        ev("tensor", lambda e: e.drain(), inc="pe")
        f2 = tok("pe")
        ev("vector", lambda e: e.tensor_reduce(srow[0:1, 0:1], pfold[0:1, 0:128],
                                               axis=AX, op=Alu.max,
                                               apply_absolute_value=True),
           waits=[f2], inc="ve")
        f3 = tok("ve")
        ev("sync", lambda e, cc_i=cc_i: e.dma_start(out=cc_i[0:1, 0:1],
                                                    in_=srow[0:1, 0:1]),
           waits=[f3], inc="ld", n=16)
        f4 = tok("ld")
        ev("gpsimd", lambda e, cc_i=cc_i, cc_o=cc_o: e.collective_compute(
            "AllGather", Alu.bypass, replica_groups=[list(range(NCORE))],
            ins=[cc_i.ap().opt()], outs=[cc_o.ap().opt()]),
           waits=[f4], inc="cc")
        f5 = tok("cc")
        ev("gpsimd", lambda e, cc_o=cc_o, sagc=sagc: e.dma_start(
            out=srow[0:1, sagc:sagc + 128], in_=cc_o[:, :]),
           waits=[f5], inc="gld", n=16)
        f6 = tok("gld")
        ev("vector", lambda e, sagc=sagc: e.tensor_reduce(
            srow[0:1, 1:2],
            srow[0:1, sagc:sagc + 128].rearrange("o (r j) -> o r j", r=8)[:, :, 0:1]
            .rearrange("o r j -> o (r j)"),
            axis=AX, op=Alu.max, apply_absolute_value=True),
           waits=[f6], inc="ve")
        gmax_t = tok("ve")
        ev("scalar", lambda e, clip_col=clip_col: e.activation(
            srow[0:1, 2:3], clips_sb[0:1, clip_col:clip_col + 1], AF.Exp,
            bias=0.0, scale=fc[0:1, 4:5]), inc="se")
        sg_t = tok("se")
        ev("vector", lambda e: e.tensor_scalar(srow[0:1, 2:3], srow[0:1, 2:3],
                                               1.0, None, Alu.add),
           waits=[sg_t], inc="ve")
        _pt = tok("ve")
        ev("vector", lambda e: e.reciprocal(srow[0:1, 2:3], srow[0:1, 2:3]),
           waits=[_pt], inc="ve")
        _rt = tok("ve")
        ev("vector", lambda e: e.tensor_tensor(srow[0:1, 3:4], srow[0:1, 2:3],
                                               srow[0:1, 1:2], Alu.mult),
           waits=[_rt, gmax_t], inc="ve")
        _tt = tok("ve")
        ev("vector", lambda e: e.tensor_scalar(srow[0:1, 4:5], srow[0:1, 3:4],
                                               1.0 / 127.0, None, Alu.mult),
           waits=[_tt], inc="ve")
        ev("vector", lambda e: e.reciprocal(srow[0:1, 5:6], srow[0:1, 3:4]), inc="ve")
        _rt2 = tok("ve")
        ev("vector", lambda e: e.tensor_scalar(srow[0:1, 5:6], srow[0:1, 5:6],
                                               127.0, None, Alu.mult),
           waits=[_rt2], inc="ve")
        sc_t = tok("ve")
        ev("tensor", lambda e: e.matmul(pfold[:, 0:2], onesf[0:1, :], srow[0:1, 4:6],
                                        start=True, stop=True),
           waits=[sc_t], inc="pe")
        bc2 = tok("pe")
        ev("vector", lambda e, invb=invb: e.tensor_copy(invb[:, :], pfold[:, 0:2]),
           waits=[bc2], inc="ve")

    fold_and_scales(amax_buf, 64, cc1_in, cc1_out, 128, 0, xinvb)
    p.barrier()
    p.flush()
    ph.close()

    ph = ExitStack()
    xint = sbt("xint", [128, 32768], BF16, ph)
    x1s = [sbt(f"x1s_{i}", [128, 4096], F32, ph) for i in range(2)]
    i32t = sbt("i32t", [128, 4096], I32, ph)
    wgs = [sbt(f"wgs_{i}", [128, 1024], F32, ph) for i in range(2)]
    wus = [sbt(f"wus_{i}", [128, 1024], F32, ph) for i in range(2)]
    wi32 = sbt("wi32", [128, 1024], I32, ph)
    wintb = sbt("wintb", [128, 2048], BF16, ph)
    wgT = [sbt(f"wgT_{i}", [128, 1024], BF16, ph) for i in range(2)]
    wuT = [sbt(f"wuT_{i}", [128, 1024], BF16, ph) for i in range(2)]
    vstage = [sbt(f"vstage_{i}", [128, 4096], F32, ph) for i in range(2)]
    silu_t = [sbt(f"silu_{i}", [128, 512], F32, ph) for i in range(2)]
    vm8 = sbt("vm8", [128, 8], F32, ph)
    pgu = [pst(f"pgu_{i}", [128, 512], F32, ph) for i in range(4)]
    ptw = pst("ptw", [128, 1024], BF16, ph)

    p.serialize = "p5" in KSER
    q_t = {}
    for dc in range(NDC):
        bi = dc % 2
        waits = []
        if dc >= 2:
            waits.append(q_t[dc - 2])
        lxs = "lxa" if dc % 2 == 0 else "lxb"
        ev("sync", lambda e, bi=bi, dc=dc: e.dma_start(out=x1s[bi][:, :],
                                                       in_=x1d[dc, :, :]),
           waits=waits, inc=lxs, n=16)
        ld_t = tok(lxs)
        w2 = [ld_t]
        if dc >= 1:
            w2.append(q_t[dc - 1])
        ev("vector", lambda e, bi=bi: e.tensor_scalar(i32t[:, :], x1s[bi][:, :],
                                                      xinvb[:, 1:2], None, Alu.mult),
           waits=w2, inc="ve")
        ev("vector", lambda e, dc=dc: e.tensor_scalar(
            xint[:, dc * 4096:(dc + 1) * 4096], i32t[:, :], 127.0, -128.0,
            Alu.min, Alu.max), inc="ve")
        q_t[dc] = tok("ve")
    xint_done = tok("ve")

    ev("sync", lambda e: e.dma_start(out=wgs[0][:, :], in_=wg_in[0:128, :]),
       inc="lwa", n=16)
    ev("sync", lambda e: e.dma_start(out=wus[0][:, :], in_=wu_in[0:128, :]),
       inc="lwa", n=16)
    wld_t = {0: tok("lwa")}

    pg_free = [None, None]
    spill_t = {}
    tq_free = {}
    silu_done = {}
    tp_guard = tok("pe")
    for ic in range(NIC):
        bi = ic % 2
        scg_col = 12 + 3 * (ic % 2)
        if ic + 1 < NIC:
            waits = []
            if (ic - 1) in tq_free:
                waits.append(tq_free[ic - 1])
            lws = "lwa" if (ic + 1) % 2 == 0 else "lwb"
            ev("sync", lambda e, ic=ic, b2=(ic + 1) % 2: e.dma_start(
                out=wgs[b2][:, :], in_=wg_in[(ic + 1) * 128:(ic + 2) * 128, :]),
               waits=waits, inc=lws, n=16)
            ev("sync", lambda e, ic=ic, b2=(ic + 1) % 2: e.dma_start(
                out=wus[b2][:, :], in_=wu_in[(ic + 1) * 128:(ic + 2) * 128, :]),
               inc=lws, n=16)
            wld_t[ic + 1] = tok(lws)

        prev_tp = tp_guard
        w_q0 = [wld_t[ic], prev_tp]
        if (ic - 2) in silu_done:
            w_q0.append(silu_done[ic - 2])
        for wi, (wstage, col0) in enumerate(((wgs[bi], 0), (wus[bi], 1024))):
            ev("vector", lambda e, wstage=wstage: e.tensor_reduce(
                ssc[:, 9:10], wstage[:, :], axis=AX, op=Alu.max,
                apply_absolute_value=True),
               waits=(w_q0 if wi == 0 else [wld_t[ic]]), inc="ve")
            _t = tok("ve")
            ev("vector", lambda e: e.reciprocal(ssc[:, 10:11], ssc[:, 9:10]),
               waits=[_t], inc="ve")
            _t = tok("ve")
            ev("vector", lambda e: e.tensor_scalar(ssc[:, 10:11], ssc[:, 10:11],
                                                   127.0, None, Alu.mult),
               waits=[_t], inc="ve")
            _t2 = tok("ve")
            ev("vector", lambda e, wstage=wstage: e.tensor_scalar(
                wi32[:, :], wstage[:, :], ssc[:, 10:11], None, Alu.mult),
               waits=[_t2], inc="ve")
            ev("vector", lambda e, col0=col0: e.tensor_scalar(
                wintb[:, col0:col0 + 1024], wi32[:, :], 127.0, -128.0,
                Alu.min, Alu.max), inc="ve")
            if wi == 0:
                ev("vector", lambda e: e.tensor_scalar(ssc[:, 11:12], ssc[:, 9:10],
                                                       1.0 / 127.0, None, Alu.mult),
                   waits=[_t], inc="ve")
                _t3 = tok("ve")
                ev("vector", lambda e, scg_col=scg_col: e.tensor_scalar(
                    ssc[:, scg_col:scg_col + 1], ssc[:, 11:12],
                    xinvb[:, 0:1], None, Alu.mult),
                   waits=[_t3], inc="ve")
            else:
                ev("vector", lambda e: e.tensor_scalar(ssc[:, 13:14], ssc[:, 9:10],
                                                       1.0 / 127.0, None, Alu.mult),
                   waits=[_t], inc="ve")
                _t3 = tok("ve")
                ev("vector", lambda e, ic=ic: e.tensor_scalar(
                    wsu_sc[:, ic:ic + 1], ssc[:, 13:14], xinvb[:, 0:1], None,
                    Alu.mult), waits=[_t3], inc="ve")
        tq_free[ic] = tok("ve")
        q_done = tok("ve")

        prev_evac = None
        for wi, (col0, dstT) in enumerate(((0, wgT[bi]), (1024, wuT[bi]))):
            w_tp = [q_done] if wi == 0 else [prev_evac]
            for hc in range(NHC):
                ev("tensor", lambda e, col0=col0, hc=hc: e.transpose(
                    ptw[:, hc * 128:(hc + 1) * 128],
                    wintb[:, col0 + hc * 128: col0 + (hc + 1) * 128], identb[:, :]),
                   waits=(w_tp if hc == 0 else ()))
            ev("tensor", lambda e: e.drain(), inc="pe")
            tp_t = tok("pe")
            ev("vector", lambda e, dstT=dstT: e.tensor_copy(dstT[:, :], ptw[:, :]),
               waits=[tp_t], inc="ve")
            prev_evac = tok("ve")
        wT_t = tok("ve")
        tp_guard = tok("pe")

        vred = []
        for tt in range(NTT):
            kg, ku = 2 * (tt % 2), 2 * (tt % 2) + 1
            w_mm = [wT_t, xint_done]
            if pg_free[tt % 2] is not None:
                w_mm.append(pg_free[tt % 2])
            if ic >= 2 and (ic - 2) in spill_t:
                w_mm.append(spill_t[ic - 2])
            for hc in range(NHC):
                ev("tensor", lambda e, kg=kg, bi=bi, hc=hc, tt=tt: e.matmul(
                    pgu[kg][:, :], wgT[bi][:, hc * 128:(hc + 1) * 128],
                    xint[:, hc * 4096 + tt * 512: hc * 4096 + (tt + 1) * 512],
                    start=(hc == 0), stop=(hc == NHC - 1)),
                   waits=(w_mm if hc == 0 else ()), inc=None)
            for hc in range(NHC):
                ev("tensor", lambda e, ku=ku, bi=bi, hc=hc, tt=tt: e.matmul(
                    pgu[ku][:, :], wuT[bi][:, hc * 128:(hc + 1) * 128],
                    xint[:, hc * 4096 + tt * 512: hc * 4096 + (tt + 1) * 512],
                    start=(hc == 0), stop=(hc == NHC - 1)),
                   inc=("pe" if hc == NHC - 1 else None))
            mm_t = tok("pe")
            ev("scalar", lambda e, kg=kg, tt=tt, scg_col=scg_col: e.activation(
                silu_t[tt % 2][:, :], pgu[kg][:, :], AF.Silu,
                bias=0.0, scale=ssc[:, scg_col:scg_col + 1]),
               waits=[mm_t], inc="se")
            sl_t = tok("se")
            ev("vector", lambda e, ku=ku, tt=tt, bi=bi: e.tensor_tensor(
                vstage[bi][:, tt * 512:(tt + 1) * 512], silu_t[tt % 2][:, :],
                pgu[ku][:, :], Alu.mult),
               waits=[sl_t], inc="ve")
            ev("vector", lambda e, tt=tt, bi=bi: e.tensor_reduce(
                vm8[:, tt:tt + 1], vstage[bi][:, tt * 512:(tt + 1) * 512],
                axis=AX, op=Alu.max, apply_absolute_value=True), inc="ve")
            pg_free[tt % 2] = tok("ve")
            vred.append(tok("ve"))
        silu_done[ic] = tok("se")
        ev("vector", lambda e: e.tensor_reduce(ssc[:, 14:15], vm8[:, :], axis=AX,
                                               op=Alu.max), waits=[vred[-1]], inc="ve")
        _t = tok("ve")
        ev("vector", lambda e, ic=ic: e.tensor_scalar(
            vmax_buf[:, ic:ic + 1], ssc[:, 14:15], wsu_sc[:, ic:ic + 1], None,
            Alu.mult), waits=[_t], inc="ve")
        v_t = tok("ve")
        sps = "spa" if ic % 2 == 0 else "spb"
        ev("gpsimd", lambda e, bi=bi, ic=ic: e.dma_start(out=vd[ic, :, :],
                                                         in_=vstage[bi][:, :]),
           waits=[v_t], inc=sps, n=16)
        spill_t[ic] = tok(sps)

    if "xint" in dbg:
        ev("gpsimd", lambda e: e.dma_start(out=dbg["xint"][:, :], in_=xint[:, :]),
           waits=[tok("ve")], inc="gld", n=16)

    p.barrier()
    p.flush()
    ph.close()

    ph = ExitStack()
    pfold = pst("pfold2", [128, 512], F32, ph)
    wdT = sbt("wdT", [128, 22528], BF16, per)
    wds = [sbt(f"wds_{i}", [128, 2816], F32, ph) for i in range(2)]
    wdi32 = sbt("wdi32", [128, 2816], I32, ph)
    wdbf = sbt("wdbf", [128, 2816], BF16, ph)
    ptw2 = pst("ptw2", [128, 1024], BF16, ph)

    def fold2(buf, ncols, cc_i, cc_o, clip_col, invb):
        ev("vector", lambda e: e.tensor_reduce(ssc[:, 8:9], buf[:, 0:ncols], axis=AX,
                                               op=Alu.max, apply_absolute_value=True),
           inc="ve")
        f1 = tok("ve")
        ev("tensor", lambda e: e.transpose(pfold[0:1, 0:128], ssc[:, 8:9], identf[:, :]),
           waits=[f1])
        ev("tensor", lambda e: e.drain(), inc="pe")
        f2 = tok("pe")
        ev("vector", lambda e: e.tensor_reduce(srow[0:1, 0:1], pfold[0:1, 0:128],
                                               axis=AX, op=Alu.max,
                                               apply_absolute_value=True),
           waits=[f2], inc="ve")
        f3 = tok("ve")
        ev("sync", lambda e, cc_i=cc_i: e.dma_start(out=cc_i[0:1, 0:1],
                                                    in_=srow[0:1, 0:1]),
           waits=[f3], inc="ld", n=16)
        f4 = tok("ld")
        ev("gpsimd", lambda e, cc_i=cc_i, cc_o=cc_o: e.collective_compute(
            "AllGather", Alu.bypass, replica_groups=[list(range(NCORE))],
            ins=[cc_i.ap().opt()], outs=[cc_o.ap().opt()]),
           waits=[f4], inc="cc")
        f5 = tok("cc")
        ev("gpsimd", lambda e, cc_o=cc_o: e.dma_start(
            out=srow[0:1, 128:256], in_=cc_o[:, :]), waits=[f5], inc="gld", n=16)
        f6 = tok("gld")
        ev("vector", lambda e: e.tensor_reduce(
            srow[0:1, 1:2],
            srow[0:1, 128:256].rearrange("o (r j) -> o r j", r=8)[:, :, 0:1]
            .rearrange("o r j -> o (r j)"),
            axis=AX, op=Alu.max, apply_absolute_value=True),
           waits=[f6], inc="ve")
        gmax_t = tok("ve")
        ev("scalar", lambda e, clip_col=clip_col: e.activation(
            srow[0:1, 2:3], clips_sb[0:1, clip_col:clip_col + 1], AF.Exp,
            bias=0.0, scale=fc[0:1, 4:5]), inc="se")
        sg_t = tok("se")
        ev("vector", lambda e: e.tensor_scalar(srow[0:1, 2:3], srow[0:1, 2:3],
                                               1.0, None, Alu.add),
           waits=[sg_t], inc="ve")
        _pt = tok("ve")
        ev("vector", lambda e: e.reciprocal(srow[0:1, 2:3], srow[0:1, 2:3]),
           waits=[_pt], inc="ve")
        _rt = tok("ve")
        ev("vector", lambda e: e.tensor_tensor(srow[0:1, 3:4], srow[0:1, 2:3],
                                               srow[0:1, 1:2], Alu.mult),
           waits=[_rt, gmax_t], inc="ve")
        _tt = tok("ve")
        ev("vector", lambda e: e.tensor_scalar(srow[0:1, 4:5], srow[0:1, 3:4],
                                               1.0 / 127.0, None, Alu.mult),
           waits=[_tt], inc="ve")
        ev("vector", lambda e: e.reciprocal(srow[0:1, 5:6], srow[0:1, 3:4]), inc="ve")
        _rt2 = tok("ve")
        ev("vector", lambda e: e.tensor_scalar(srow[0:1, 5:6], srow[0:1, 5:6],
                                               127.0, None, Alu.mult),
           waits=[_rt2], inc="ve")
        sc_t = tok("ve")
        ev("tensor", lambda e: e.matmul(pfold[:, 0:2], onesf[0:1, :], srow[0:1, 4:6],
                                        start=True, stop=True),
           waits=[sc_t], inc="pe")
        bc2 = tok("pe")
        ev("vector", lambda e, invb=invb: e.tensor_copy(invb[:, :], pfold[:, 0:2]),
           waits=[bc2], inc="ve")

    fold2(vmax_buf, NIC, cc2_in, cc2_out, 1, hinvb)
    hs_t = tok("ve")
    ev("vector", lambda e: e.tensor_scalar(qscale_g[:, 0:NIC], wsu_sc[:, 0:NIC],
                                           hinvb[:, 1:2], None, Alu.mult),
       waits=[hs_t], inc="ve")

    if "scales" in dbg:
        ev("sync", lambda e: e.dma_start(out=dbg["scales"][:, 0:24], in_=wsu_sc[:, :]),
           waits=[tok("ve")], inc="ld", n=16)
        ev("sync", lambda e: e.dma_start(out=dbg["scales"][:, 24:48], in_=qscale_g[:, :]),
           inc="ld", n=16)
        ev("sync", lambda e: e.dma_start(out=dbg["scales"][:, 48:50], in_=xinvb[:, :]),
           inc="ld", n=16)
        ev("sync", lambda e: e.dma_start(out=dbg["scales"][:, 50:52], in_=hinvb[:, :]),
           inc="ld", n=16)
        ev("sync", lambda e: e.dma_start(out=dbg["scales"][:, 52:60], in_=scd_g[:, :]),
           inc="ld", n=16)

    ev("sync", lambda e: e.dma_start(out=wds[0][:, :], in_=wd_in[0:128, :]),
       inc="lwa", n=16)
    wld = {0: tok("lwa")}
    tqf = {}
    for dcb in range(NDC):
        bi = dcb % 2
        if dcb + 1 < NDC:
            waits = []
            if (dcb - 1) in tqf:
                waits.append(tqf[dcb - 1])
            lws = "lwa" if (dcb + 1) % 2 == 0 else "lwb"
            ev("sync", lambda e, dcb=dcb, b2=(dcb + 1) % 2: e.dma_start(
                out=wds[b2][:, :], in_=wd_in[(dcb + 1) * 128:(dcb + 2) * 128, :]),
               waits=waits, inc=lws, n=16)
            wld[dcb + 1] = tok(lws)
        prev_tp = tok("pe")
        ev("vector", lambda e, bi=bi: e.tensor_reduce(
            ssc[:, 9:10], wds[bi][:, :], axis=AX, op=Alu.max,
            apply_absolute_value=True), waits=[wld[dcb], prev_tp], inc="ve")
        _t = tok("ve")
        ev("vector", lambda e: e.reciprocal(ssc[:, 10:11], ssc[:, 9:10]),
           waits=[_t], inc="ve")
        _t = tok("ve")
        ev("vector", lambda e: e.tensor_scalar(ssc[:, 10:11], ssc[:, 10:11],
                                               127.0, None, Alu.mult),
           waits=[_t], inc="ve")
        _t2 = tok("ve")
        ev("vector", lambda e, bi=bi: e.tensor_scalar(
            wdi32[:, :], wds[bi][:, :], ssc[:, 10:11], None, Alu.mult),
           waits=[_t2], inc="ve")
        ev("vector", lambda e: e.tensor_scalar(wdbf[:, :], wdi32[:, :],
                                               127.0, -128.0, Alu.min, Alu.max),
           inc="ve")
        ev("vector", lambda e: e.tensor_scalar(ssc[:, 11:12], ssc[:, 9:10],
                                               1.0 / 127.0, None, Alu.mult),
           waits=[_t], inc="ve")
        _t3 = tok("ve")
        ev("vector", lambda e, dcb=dcb: e.tensor_scalar(
            scd_g[:, dcb:dcb + 1], ssc[:, 11:12], hinvb[:, 0:1], None, Alu.mult),
           waits=[_t3], inc="ve")
        tqf[dcb] = tok("ve")
        q_done = tok("ve")
        done = 0
        prev_evac2 = None
        while done < NIC:
            grp = min(8, NIC - done)
            w_tp = [q_done] if prev_evac2 is None else [prev_evac2]
            for j in range(grp):
                icx = done + j
                ev("tensor", lambda e, icx=icx, j=j: e.transpose(
                    ptw2[:, j * 128:(j + 1) * 128],
                    wdbf[:, icx * 128:(icx + 1) * 128], identb[:, :]),
                   waits=(w_tp if j == 0 else ()))
            ev("tensor", lambda e: e.drain(), inc="pe")
            tp_t = tok("pe")
            ev("vector", lambda e, done=done, grp=grp, dcb=dcb: e.tensor_copy(
                wdT[:, :].rearrange("p (i c d) -> p i c d", c=8, d=128)
                [:, done:done + grp, dcb:dcb + 1, :],
                ptw2[:, 0:grp * 128].rearrange("p (i d) -> p i d", d=128)
                [:, :, None, :] if False else
                ptw2[:, 0:grp * 128].rearrange("p (i o d) -> p i o d", o=1, d=128)),
               waits=[tp_t], inc="ve")
            prev_evac2 = tok("ve")
            done += grp

    p.barrier()
    p.flush()
    ph.close()

    ph = ExitStack()
    hint = [sbt(f"hint_{i}", [128, 11264], BF16, ph) for i in range(2)]
    vrs = [sbt(f"vrs_{i}", [128, 512], F32, ph) for i in range(4)]
    vi32 = [sbt(f"vi32_{i}", [128, 512], I32, ph) for i in range(2)]
    x1r = [sbt(f"x1r_{i}", [128, 512], F32, ph) for i in range(2)]
    x2t = [sbt(f"x2t_{i}", [128, 4096], F32, ph) for i in range(2)]
    rb2 = sbt("rb2", [128, 4096], BF16, ph)
    rsq2 = [sbt(f"rsq2_{i}", [128, 512], BF16, ph) for i in range(3)]
    ostage = [sbt(f"ostage_{i}", [128, 1024], F32, ph) for i in range(2)]
    pd = [pst(f"pd_{i}", [128, 512], F32, ph) for i in range(2)]
    pm = pst("pm7", [1, 512], F32, ph)
    ps2 = pst("ps27", [1, 512], F32, ph)
    pbc_nm = pst("pbc_nm7", [128, 512], F32, ph)
    pbc_rs = pst("pbc_rs7", [128, 512], F32, ph)
    ptr7 = [pst(f"ptr7_{i}", [128, 512], F32, ph) for i in range(2)]

    hq_t = {}
    pd_free = [None, None]
    bc_free = None
    stats_pend7 = []
    out_pend = []
    tr_bank = {}
    x2_freed = {}

    def flush_out_group():
        xb2, tt0, tc, g, ln_tok = out_pend.pop(0)
        gb = (tt0 * 8 + tc * 2 + g) % 2
        ob = tc % 2
        w_tr = [ln_tok]
        if gb in tr_bank:
            w_tr.append(tr_bank[gb])
        if g == 0 and ost_free[ob] is not None:
            w_tr.append(ost_free[ob])
        for j in range(4):
            dc = g * 4 + j
            ev("tensor", lambda e, gb=gb, dc=dc, tc=tc, j=j, xb2=xb2: e.transpose(
                ptr7[gb][:, j * 128:(j + 1) * 128],
                x2t[xb2][:, dc * 512 + tc * 128: dc * 512 + (tc + 1) * 128],
                identf[:, :]),
               waits=(w_tr if j == 0 else ()))
        ev("tensor", lambda e: e.drain(), inc="pe")
        tp_t = tok("pe")
        ev("vector", lambda e, gb=gb, g=g, ob=ob: e.tensor_copy(
            ostage[ob][:, g * 512:(g + 1) * 512], ptr7[gb][:, :]),
           waits=[tp_t], inc="ve")
        tr_bank[gb] = tok("ve")
        if g == 1:
            sts = "sta" if ob == 0 else "stb"
            ev("sync", lambda e, ob=ob, tt0=tt0, tc=tc: e.dma_start(
                out=out[tt0 * 512 + tc * 128: tt0 * 512 + (tc + 1) * 128, :],
                in_=ostage[ob][:, :]),
               waits=[tr_bank[gb]], inc=sts, n=16)
            ost_free[ob] = tok(sts)
        if tc == 3 and g == 1:
            x2_freed[xb2] = tok("ve")

    def flush_stats7(bcf):
        for (dcx, ct2, sqt) in stats_pend7:
            w_m1 = [ct2]
            if dcx == 0 and bcf is not None:
                w_m1.append(bcf)
            ev("tensor", lambda e, dcx=dcx: e.matmul(
                pm[:, :], onesb[:, 0:1], rb2[:, dcx * 512:(dcx + 1) * 512],
                start=(dcx == 0), stop=(dcx == NDC - 1)), waits=w_m1, inc=None)
            ev("tensor", lambda e, dcx=dcx: e.matmul(
                ps2[:, :], onesb[:, 0:1], rsq2[dcx % 3][:, :],
                start=(dcx == 0), stop=(dcx == NDC - 1)), waits=[sqt], inc="pe")
        stats_pend7.clear()
    ost_free = [None, None]
    stt_t = {}
    mmdone_t = {}

    p.serialize = "p7" in KSER
    for tt in range(NTT):
        hb = tt % 2
        for ic in range(NIC):
            vb = (tt * NIC + ic) % 4
            waits = []
            if (tt * NIC + ic) >= 4:
                waits.append(hq_t.get((tt * NIC + ic - 4), tok("ve")))
            if tt >= 2 and ic == 0:
                waits.append(hq_t[(tt - 2, "done")])
            lvs = f"lv{(tt * NIC + ic) % 4}"
            ev("sync", lambda e, ic=ic, tt=tt, vb=vb: e.dma_start(
                out=vrs[vb][:, :], in_=vd[ic, :, tt * 512:(tt + 1) * 512]),
               waits=waits, inc=lvs, n=16)
            ld_t = tok(lvs)
            w2 = [ld_t]
            if (tt * NIC + ic) >= 2:
                w2.append(hq_t.get((tt * NIC + ic - 2), tok("ve")))
            if tt >= 2 and ic == 0:
                w2.append(mmdone_t[(tt - 2)] if (tt - 2) in mmdone_t else tok("pe"))
            vib = ic % 2
            ev("vector", lambda e, ic=ic, vb=vb, vib=vib: e.tensor_scalar(
                vi32[vib][:, :], vrs[vb][:, :], qscale_g[:, ic:ic + 1], None,
                Alu.mult), waits=w2, inc="ve")
            vq_t = tok("ve")
            clip_eng = "gpsimd" if ic % 2 == 0 else "vector"
            ev(clip_eng, lambda e, ic=ic, hb=hb, vib=vib: e.tensor_scalar(
                hint[hb][:, ic * 512:(ic + 1) * 512], vi32[vib][:, :],
                127.0, -128.0, Alu.min, Alu.max),
               waits=[vq_t], inc=("ge" if ic % 2 == 0 else "ve"))
            hq_t[(tt * NIC + ic)] = tok("ge" if ic % 2 == 0 else "ve")
        hq_done = (tok("ge"), tok("ve"))
        hq_t[(tt, "done")] = hq_done[0]
        hint_rdy = hq_done

        for dc in range(NDC):
            gi = tt * NDC + dc
            xb = gi % 2
            waits = []
            if gi >= 2:
                waits.append(stt_t[gi - 2])
            lxs = "lxa" if gi % 2 == 0 else "lxb"
            ev("sync", lambda e, dc=dc, tt=tt, xb=xb: e.dma_start(
                out=x1r[xb][:, :], in_=x1d[dc, :, tt * 512:(tt + 1) * 512]),
               waits=waits, inc=lxs, n=16)
            x1ld = tok(lxs)
            k = dc % 2
            w_mm = list(hint_rdy)
            if pd_free[k] is not None:
                w_mm.append(pd_free[k])
            for ic in range(NIC):
                ev("tensor", lambda e, k=k, ic=ic, dc=dc, hb=hb: e.matmul(
                    pd[k][:, :], wdT[:, (ic * 8 + dc) * 128:(ic * 8 + dc + 1) * 128],
                    hint[hb][:, ic * 512:(ic + 1) * 512],
                    start=(ic == 0), stop=(ic == NIC - 1)),
                   waits=(w_mm if ic == 0 else ()),
                   inc=("pe" if ic == NIC - 1 else None))
            mm_t = tok("pe")
            flush_stats7(bc_free)
            if out_pend:
                flush_out_group()
            w_stt = [mm_t, x1ld]
            if dc == 0 and (tt % 2) in x2_freed:
                w_stt.append(x2_freed[tt % 2])
            ev("vector", lambda e, k=k, dc=dc, xb=xb, xb2=tt % 2: e.scalar_tensor_tensor(
                x2t[xb2][:, dc * 512:(dc + 1) * 512], pd[k][:, :], scd_g[:, dc:dc + 1],
                x1r[xb][:, :], Alu.mult, Alu.add),
               waits=w_stt, inc="ve")
            pd_free[k] = tok("ve")
            stt_t[gi] = tok("ve")
            r2_t = tok("ve")
            ev("scalar", lambda e, dc=dc, xb2=tt % 2: e.copy(
                rb2[:, dc * 512:(dc + 1) * 512], x2t[xb2][:, dc * 512:(dc + 1) * 512]),
               waits=[r2_t], inc="se")
            cast_t = tok("se")
            ev("gpsimd", lambda e, dc=dc: e.tensor_tensor(
                rsq2[dc % 3][:, :], rb2[:, dc * 512:(dc + 1) * 512],
                rb2[:, dc * 512:(dc + 1) * 512], Alu.mult),
               waits=([cast_t] if dc < 3 else [cast_t, tok("pe")]), inc="ge")
            sq_t = tok("ge")
            stats_pend7.append((dc, cast_t, sq_t))
        flush_stats7(bc_free)
        mmdone_t[tt] = tok("pe")
        st_mm = tok("pe")
        ev("vector", lambda e: e.tensor_scalar(srow[0:1, 0:512], pm[:, :],
                                               1.0 / D, None, Alu.mult),
           waits=[st_mm], inc="ve")
        ev("vector", lambda e: e.tensor_tensor(srow[0:1, 512:1024], srow[0:1, 0:512],
                                               srow[0:1, 0:512], Alu.mult), inc="ve")
        ev("vector", lambda e: e.scalar_tensor_tensor(
            srow[0:1, 1024:1536], ps2[:, :], 1.0 / D, srow[0:1, 512:1024],
            Alu.mult, Alu.subtract), inc="ve")
        ev("vector", lambda e: e.tensor_scalar(srow[0:1, 1536:2048], srow[0:1, 0:512],
                                               -1.0, None, Alu.mult), inc="ve")
        var_t = tok("ve")
        ev("scalar", lambda e: e.activation(srow[0:1, 1024:1536], srow[0:1, 1024:1536],
                                            AF.Ln, bias=fc[0:1, 1:2], scale=1.0),
           waits=[var_t], inc="se")
        ev("scalar", lambda e: e.activation(srow[0:1, 1024:1536], srow[0:1, 1024:1536],
                                            AF.Exp, bias=0.0, scale=fc[0:1, 3:4]), inc="se")
        rstd_t = tok("se")
        ev("tensor", lambda e: e.matmul(pbc_nm[:, :], onesf[0:1, :],
                                        srow[0:1, 1536:2048], start=True, stop=True),
           waits=[rstd_t], inc="pe")
        ev("tensor", lambda e: e.matmul(pbc_rs[:, :], onesf[0:1, :],
                                        srow[0:1, 1024:1536], start=True, stop=True),
           inc="pe")
        bct = tok("pe")
        ln_t = {}
        for dc in range(NDC):
            ev("vector", lambda e, dc=dc, xb2=tt % 2: e.tensor_tensor(
                x2t[xb2][:, dc * 512:(dc + 1) * 512],
                x2t[xb2][:, dc * 512:(dc + 1) * 512],
                pbc_nm[:, :], Alu.add), waits=[bct] if dc == 0 else (), inc=None)
            ev("vector", lambda e, dc=dc, xb2=tt % 2: e.tensor_tensor(
                x2t[xb2][:, dc * 512:(dc + 1) * 512],
                x2t[xb2][:, dc * 512:(dc + 1) * 512],
                pbc_rs[:, :], Alu.mult), inc="ve")
            ln_t[dc] = tok("ve")
        bc_free = tok("ve")

        lnt_last = ln_t[NDC - 1]
        for tc in range(4):
            for g in range(2):
                out_pend.append((tt % 2, tt, tc, g, lnt_last))
        if tt == NTT - 1:
            while out_pend:
                flush_out_group()

    ev("sync", lambda e: None, waits=(("sta", cnt["sta"]), ("stb", cnt["stb"]),
                                      ("st", cnt["st"])))
    p.barrier()
    p.flush()
    ph.close()
    per.close()
    p.es.close()
    return nc


def _consts():
    identf = np.eye(128, dtype=np.float32)
    identb = np.eye(128, dtype=bf)
    onesf = np.ones((128, 128), dtype=np.float32)
    onesb = np.ones((128, 8), dtype=bf)
    hsel = np.zeros((128, 128), dtype=np.float32)
    for hp in range(NHC):
        for j in range(16):
            if j // 2 == hp:
                hsel[(j % 2) * 64:(j % 2 + 1) * 64, hp * 16 + j] = 1.0
    hsel = hsel.astype(bf)
    maskb = (np.arange(128)[None, :] >= np.arange(128)[:, None]).astype(bf)
    fconst = np.zeros((128, 8), dtype=np.float32)
    fconst[:, 0] = 1e-20
    fconst[:, 1] = 1e-5
    fconst[:, 2] = 0.5
    fconst[:, 3] = -0.5
    fconst[:, 4] = -1.0
    return identf, identb, onesf, onesb, hsel, maskb, fconst


_NC_CACHE = {}


def kernel(L0_attn_x, L0_attn_Wq, L0_attn_Wk, L0_attn_Wv, L0_attn_Wout,
           L0_ffn_clip_x, L0_ffn_clip_down, L0_ffn_w_gate, L0_ffn_w_up,
           L0_ffn_w_down):
    x = np.ascontiguousarray(np.asarray(L0_attn_x, dtype=np.float32))
    wq = np.ascontiguousarray(np.asarray(L0_attn_Wq, dtype=np.float32).reshape(D, D))
    wk = np.ascontiguousarray(np.asarray(L0_attn_Wk, dtype=np.float32).reshape(D, D))
    wv = np.ascontiguousarray(np.asarray(L0_attn_Wv, dtype=np.float32).reshape(D, D))
    wo = np.ascontiguousarray(np.asarray(L0_attn_Wout, dtype=np.float32).reshape(D, D))
    wg = np.ascontiguousarray(np.asarray(L0_ffn_w_gate, dtype=np.float32))
    wu = np.ascontiguousarray(np.asarray(L0_ffn_w_up, dtype=np.float32))
    wd = np.ascontiguousarray(np.asarray(L0_ffn_w_down, dtype=np.float32))
    clips = np.array([[float(np.asarray(L0_ffn_clip_x).reshape(-1)[0]),
                       float(np.asarray(L0_ffn_clip_down).reshape(-1)[0])]],
                     dtype=np.float32)
    identf, identb, onesf, onesb, hsel, maskb, fconst = _consts()

    if "nc" not in _NC_CACHE:
        _NC_CACHE["nc"] = build()
    nc = _NC_CACHE["nc"]

    base = {"wq": wq, "wk": wk, "wv": wv, "wo": wo, "wg": wg, "wu": wu,
            "wd": wd, "clips": clips, "identf": identf, "identb": identb,
            "onesf": onesf, "onesb": onesb, "hsel": hsel, "maskb": maskb,
            "fconst": fconst}
    in_maps = []
    for c in range(NCORE):
        m = dict(base)
        m["x"] = np.ascontiguousarray(
            x.reshape(B, L, D)[2 * c:2 * c + 2].reshape(T, D))
        in_maps.append(m)
    res = run_bass_kernel_spmd(nc, in_maps, core_ids=list(range(NCORE)))
    kernel._last_results = res.results
    outs = [res.results[c]["out"] for c in range(NCORE)]
    return np.concatenate(outs, axis=0).reshape(B, L, D).astype(np.float32)


# version 15
# speedup vs baseline: 1.0100x; 1.0100x over previous
import os
import numpy as np
import ml_dtypes
from contextlib import ExitStack

import concourse.bass as bass
import concourse.mybir as mybir
from concourse.alu_op_type import AluOpType
from concourse.bass_utils import run_bass_kernel_spmd

F32 = mybir.dt.float32
BF16 = mybir.dt.bfloat16
I32 = mybir.dt.int32
AF = mybir.ActivationFunctionType
AX = mybir.AxisListType.X
Alu = AluOpType
bf = ml_dtypes.bfloat16

B, L, D, H, DH, II = 16, 2048, 1024, 16, 64, 2816
NCORE = 8
NSEQ = 2
T = NSEQ * L
NHC = 8
NDC = 8
NIC = 22
NTT = T // 512

ENGS = ("tensor", "vector", "scalar", "gpsimd", "sync")


class P:

    VDRAIN = bool(int(os.environ.get("KVDRAIN", "0")))

    def __init__(self, nc):
        self.nc = nc
        self.steps = {e: [] for e in ENGS}
        self.cnt = {}
        self.semh = {}
        self.last_wait = {e: {} for e in ENGS}
        self.es = ExitStack()
        self.nphase = 0

    def sem(self, name):
        self.semh[name] = self.es.enter_context(self.nc.semaphore(name))
        self.cnt[name] = 0

    serialize = False

    def ev(self, eng, fn, waits=(), inc=None, n=1):
        for s, v in waits:
            assert v <= self.cnt[s], f"wait on future {s} {v} > {self.cnt[s]}"
        if self.serialize:
            waits = tuple((s, self.cnt[s]) for s in
                          ("ld", "gld", "sp", "st", "cc", "pe", "ve", "se",
                           "lxa", "lxb", "lwa", "lwb", "spa", "spb",
                           "lv0", "lv1", "lv2", "lv3", "sta", "stb"))
        if inc is not None:
            self.cnt[inc] += n
        self.steps[eng].append((fn, tuple(waits), inc, n))

    def tok(self, s):
        return (s, self.cnt[s])

    def flush(self):
        nc = self.nc
        with nc.Block() as block:
            for e in ENGS:
                items = self.steps[e]
                self.steps[e] = []
                if not items:
                    continue
                lw = self.last_wait[e]

                def body(eng, items=items, lw=lw, e_name=e):
                    for fn, waits, inc, n in items:
                        for s, v in waits:
                            if lw.get(s, -1) >= v:
                                continue
                            eng.wait_ge(self.semh[s], v)
                            lw[s] = v
                        ins = fn(eng)
                        if inc is not None:
                            if (e_name == "vector" and ins is not None
                                    and P.VDRAIN):
                                ins = eng.drain()
                            if ins is None:
                                ins = eng.nop()
                            ins.then_inc(self.semh[inc], n)

                getattr(block, e)(body)

    def barrier(self):
        c = self.cnt
        self.ev("sync", lambda e: None,
                waits=tuple((s, c[s]) for s in
                            ("ld", "st", "lxa", "lxb", "lwa", "lwb",
                             "lv0", "lv1", "lv2", "lv3", "sta", "stb",
                             "rc0", "rc1")))
        self.ev("gpsimd", lambda e: None,
                waits=(("gld", c["gld"]), ("sp", c["sp"]), ("cc", c["cc"]),
                       ("spa", c["spa"]), ("spb", c["spb"])))
        for e in ENGS:
            self.ev(e, lambda en: en.nop(), inc="ph")
        self.nphase += 1
        tgt = 5 * self.nphase
        assert self.cnt["ph"] == tgt
        for e in ENGS:
            self.ev(e, lambda en: None, waits=(("ph", tgt),))


def build():
    nc = bass.Bass("TRN2", target_bir_lowering=False)

    x_in = nc.dram_tensor("x", [T, D], F32, kind="ExternalInput")
    wq_in = nc.dram_tensor("wq", [D, D], F32, kind="ExternalInput")
    wk_in = nc.dram_tensor("wk", [D, D], F32, kind="ExternalInput")
    wv_in = nc.dram_tensor("wv", [D, D], F32, kind="ExternalInput")
    wo_in = nc.dram_tensor("wo", [D, D], F32, kind="ExternalInput")
    wg_in = nc.dram_tensor("wg", [II, D], F32, kind="ExternalInput")
    wu_in = nc.dram_tensor("wu", [II, D], F32, kind="ExternalInput")
    wd_in = nc.dram_tensor("wd", [D, II], F32, kind="ExternalInput")
    clips_in = nc.dram_tensor("clips", [1, 2], F32, kind="ExternalInput")
    identf_in = nc.dram_tensor("identf", [128, 128], F32, kind="ExternalInput")
    identb_in = nc.dram_tensor("identb", [128, 128], BF16, kind="ExternalInput")
    onesf_in = nc.dram_tensor("onesf", [128, 128], F32, kind="ExternalInput")
    onesb_in = nc.dram_tensor("onesb", [128, 8], BF16, kind="ExternalInput")
    hsel_in = nc.dram_tensor("hsel", [128, 128], BF16, kind="ExternalInput")
    maskb_in = nc.dram_tensor("maskb", [128, 128], BF16, kind="ExternalInput")
    fconst_in = nc.dram_tensor("fconst", [128, 8], F32, kind="ExternalInput")
    out = nc.dram_tensor("out", [T, D], F32, kind="ExternalOutput")

    x1d = nc.dram_tensor("x1d", [NDC, 128, T], F32)
    vd = nc.dram_tensor("vd", [NIC, 128, T], F32)
    bounce = nc.dram_tensor("bounce", [1, 64], F32)
    rc_dram = nc.dram_tensor("rc_dram", [2, 2, 1024], F32)
    cc1_in = nc.dram_tensor("cc1_in", [1, 16], F32)
    cc1_out = nc.dram_tensor("cc1_out", [1, 128], F32)
    cc2_in = nc.dram_tensor("cc2_in", [1, 16], F32)
    cc2_out = nc.dram_tensor("cc2_out", [1, 128], F32)

    KDBG = os.environ.get("KDBG", "")
    KSER = set(x for x in os.environ.get("KSER", "").split(",") if x)
    dbg = {}
    if "x1" in KDBG:
        dbg["x1"] = nc.dram_tensor("dbg_x1", [NSEQ, 128, 16384], F32,
                                   kind="ExternalOutput")
    if "qkv" in KDBG:
        dbg["qt"] = nc.dram_tensor("dbg_qt", [NSEQ, 128, 16384], F32,
                                   kind="ExternalOutput")
        dbg["kt"] = nc.dram_tensor("dbg_kt", [NSEQ, 128, 16384], F32,
                                   kind="ExternalOutput")
        dbg["vh"] = nc.dram_tensor("dbg_vh", [NSEQ, 128, 16640], F32,
                                   kind="ExternalOutput")
        dbg["negc"] = nc.dram_tensor("dbg_negc", [NSEQ, 128, 32], F32,
                                     kind="ExternalOutput")
    if "negc2" in KDBG:
        dbg["qnorm"] = nc.dram_tensor("dbg_qnorm", [NSEQ, 16, 2048], F32,
                                      kind="ExternalOutput")
        dbg["ssc"] = nc.dram_tensor("dbg_ssc", [NSEQ, 128, 32], F32,
                                    kind="ExternalOutput")
        dbg["sflat"] = nc.dram_tensor("dbg_sflat", [NSEQ, 1, 64], F32,
                                      kind="ExternalOutput")
        dbg["srow32"] = nc.dram_tensor("dbg_srow32", [NSEQ, 1, 32], F32,
                                       kind="ExternalOutput")
    if "ot" in KDBG:
        dbg["ot"] = nc.dram_tensor("dbg_ot", [NSEQ, 128, 16384], F32,
                                   kind="ExternalOutput")
    if "ffn" in KDBG:
        dbg["xint"] = nc.dram_tensor("dbg_xint", [128, 32768], F32,
                                     kind="ExternalOutput")
        dbg["scales"] = nc.dram_tensor("dbg_scales", [128, 96], F32,
                                       kind="ExternalOutput")

    p = P(nc)
    for s in ("ld", "gld", "sp", "st", "cc", "pe", "ve", "se", "ge", "ph",
              "lxa", "lxb", "lwa", "lwb", "spa", "spb",
              "lv0", "lv1", "lv2", "lv3", "sta", "stb", "rc0", "rc1"):
        p.sem(s)
    ev, tok, cnt = p.ev, p.tok, p.cnt

    per = ExitStack()

    _uniq = [0]

    def sbt(name, shape, dt=F32, stack=None, side=None):
        kw = {"side": side} if side else {}
        _uniq[0] += 1
        return (stack or per).enter_context(
            nc.sbuf_tensor(f"{name}__{_uniq[0]}", shape, dt, **kw))

    def pst(name, shape, dt=F32, stack=None):
        _uniq[0] += 1
        return (stack or per).enter_context(
            nc.psum_tensor(f"{name}__{_uniq[0]}", shape, dt))

    identf = sbt("s_identf", [128, 128])
    identb = sbt("s_identb", [128, 128], BF16)
    onesf = sbt("s_onesf", [128, 128])
    onesb = sbt("s_onesb", [128, 8], BF16)
    hsel = sbt("s_hsel", [128, 128], BF16)
    maskb = sbt("s_maskb", [128, 128], BF16)
    ssc = sbt("ssc", [128, 32])
    amax_buf = sbt("amax_buf", [128, 64])
    vmax_buf = sbt("vmax_buf", [128, 24])
    wsu_sc = sbt("wsu_sc", [128, 24])
    qscale_g = sbt("qscale_g", [128, 24])
    scd_g = sbt("scd_g", [128, 8])
    xinvb = sbt("xinvb", [128, 2])
    hinvb = sbt("hinvb", [128, 2])
    clips_sb = sbt("clips_sb", [1, 2])
    fc = sbt("s_fconst", [128, 8])
    srow = sbt("srow", [128, 2048])
    sflat = sbt("sflat", [1, 64])

    for dst, src in ((identf, identf_in), (identb, identb_in), (onesf, onesf_in),
                     (onesb, onesb_in), (hsel, hsel_in), (maskb, maskb_in),
                     (clips_sb, clips_in), (fc, fconst_in)):
        ev("sync", lambda e, d=dst, s=src: e.dma_start(out=d[:, :], in_=s[:, :]),
           inc="ld", n=16)
    ev("vector", lambda e: e.memset(vmax_buf[:, :], 0.0), inc="ve")
    ev("vector", lambda e: e.memset(amax_buf[:, :], 0.0), inc="ve")
    p.barrier()
    p.flush()

    seq_es = ExitStack()

    def attention_seq(s):
        ph = ExitStack()
        att_es = ExitStack()
        Qt = sbt("Qt", [128, 16384], BF16, att_es)
        Kt = sbt("Kt", [128, 16384], BF16, att_es)
        Vhat = sbt("Vhat", [128, 16640], BF16, att_es)
        negc_b = sbt("negc_b", [128, 32], F32, att_es)
        xTbf = sbt("xTbf", [128, 16384], BF16, ph)
        wqb = sbt("wqb", [128, 8192], BF16, ph)
        wkb = sbt("wkb", [128, 8192], BF16, ph)
        wvb = sbt("wvb", [128, 8192], BF16, ph)
        xsb = [sbt(f"xsb_{i}", [128, 1024], BF16, ph) for i in range(2)]
        xs32p = [sbt("xs32p_0", [128, 1024], F32, ph)] * 2
        qsq = [sbt(f"qsq_{i}", [128, 512], BF16, ph) for i in range(2)]
        qnorm = sbt("qnorm", [16, 2048], F32, ph)
        ktmp = sbt("ktmp", [16, 512], F32, ph)
        ptrb = [pst(f"ptrb_{i}", [128, 1024], BF16, ph) for i in range(2)]
        psq = [pst(f"psq_{i}", [128, 512], F32, ph) for i in range(3)]
        pnorm = pst("pnorm", [16, 512], F32, ph)
        pbc = pst("pbc", [128, 512], F32, ph)

        p.serialize = "p1" in KSER
        for wsb, wdr in ((wqb, wq_in), (wkb, wk_in), (wvb, wv_in)):
            for dc in range(NDC):
                ev("gpsimd", lambda e, d=wsb, ss=wdr, dc=dc:
                   e.dma_start(out=d[:, dc * 1024:(dc + 1) * 1024],
                               in_=ss[dc * 128:(dc + 1) * 128, :]),
                   inc="gld", n=16)
        w_tok = tok("gld")

        p.serialize = ("p1" in KSER) or ("p1a" in KSER)
        KVAR = os.environ.get("KVAR", "")
        cast_t = {}
        evac_t = {}
        for tc in range(16):
            bi = tc % 2
            if "synccast" in KVAR:
                waits = []
                if tc >= 1:
                    waits.append(cast_t[tc - 1])
                lxs = "lxa" if tc % 2 == 0 else "lxb"
                ev("sync", lambda e, bi=bi, tc=tc:
                   e.dma_start(out=xs32p[bi][:, :],
                               in_=x_in[s * L + tc * 128: s * L + (tc + 1) * 128, :]),
                   waits=waits, inc=lxs, n=16)
                ldf = tok(lxs)
                wc = [ldf]
                if tc >= 2:
                    wc.append(evac_t[tc - 2])
                ev("vector", lambda e, bi=bi: e.tensor_copy(xsb[bi][:, :],
                                                            xs32p[bi][:, :]),
                   waits=wc, inc="ve")
                cast_t[tc] = tok("ve")
                ld_t = cast_t[tc]
            else:
                waits = []
                if tc >= 2:
                    waits.append(evac_t[tc - 2])
                lxs = "lxa" if tc % 2 == 0 else "lxb"
                ev("gpsimd", lambda e, bi=bi, tc=tc:
                   e.dma_start(out=xsb[bi][:, :],
                               in_=x_in[s * L + tc * 128: s * L + (tc + 1) * 128, :]),
                   waits=waits, inc=lxs, n=16)
                ld_t = tok(lxs)
            wv_ = [ld_t]
            if tc >= 2:
                wv_.append(evac_t[tc - 2])
            for j in range(8):
                ev("tensor", lambda e, bi=bi, j=j:
                   e.transpose(ptrb[bi][:, j * 128:(j + 1) * 128],
                               xsb[bi][:, j * 128:(j + 1) * 128], identb[:, :]),
                   waits=(wv_ if j == 0 else ()))
            ev("tensor", lambda e: e.drain(), inc="pe")
            tr_t = tok("pe")
            ev("vector", lambda e, bi=bi, tc=tc: e.tensor_copy(
                xTbf[:, :].rearrange("p (c t) -> p c t", c=8)[:, :, tc * 128:(tc + 1) * 128],
                ptrb[bi][:, :].rearrange("p (c t) -> p c t", c=8)),
               waits=[tr_t], inc="ve")
            evac_t[tc] = tok("ve")
        xT_done = tok("ve")

        p.serialize = ("p1" in KSER) or ("p1b" in KSER)
        pq_free = [None, None, None]

        def proj_qk(wsb, dst):
            nonlocal pq_free
            for hc in range(NHC):
                for tt in range(4):
                    k = (hc * 4 + tt) % 3
                    waits = [xT_done, w_tok]
                    if pq_free[k] is not None:
                        waits.append(pq_free[k])
                    for dc in range(NDC):
                        ev("tensor", lambda e, k=k, dc=dc, hc=hc, tt=tt, wsb=wsb:
                           e.matmul(psq[k][:, :],
                                    wsb[:, dc * 1024 + hc * 128: dc * 1024 + (hc + 1) * 128],
                                    xTbf[:, dc * 2048 + tt * 512: dc * 2048 + (tt + 1) * 512],
                                    start=(dc == 0), stop=(dc == NDC - 1)),
                           waits=(waits if dc == 0 else ()),
                           inc=("pe" if dc == NDC - 1 else None))
                    mm_t = tok("pe")
                    if dst is Qt:
                        ev("vector", lambda e, k=k, hc=hc, tt=tt, dst=dst:
                           e.tensor_scalar(dst[:, hc * 2048 + tt * 512: hc * 2048 + (tt + 1) * 512],
                                           psq[k][:, :], 0.125, None, Alu.mult),
                           waits=[mm_t], inc="ve")
                    else:
                        ev("vector", lambda e, k=k, hc=hc, tt=tt, dst=dst:
                           e.tensor_copy(dst[:, hc * 2048 + tt * 512: hc * 2048 + (tt + 1) * 512],
                                         psq[k][:, :]),
                           waits=[mm_t], inc="ve")
                    pq_free[k] = tok("ve")

        proj_qk(wqb, Qt)
        q_done_ve = tok("ve")
        p.serialize = ("p1" in KSER) or ("p1c" in KSER)
        proj_qk(wkb, Kt)
        k_done_ve = tok("ve")

        p.serialize = ("p1" in KSER) or ("p1d" in KSER)
        for tc in range(16):
            ev("vector", lambda e, tc=tc: e.memset(
                Vhat[:, tc * 1040:(tc + 1) * 1040]
                .rearrange("p (h e) -> p h e", h=16)[:, :, 64:65], 1.0),
               waits=([] if tc else [w_tok]), inc="ve")
            ones_t = tok("ve")
            for half in range(2):
                k = (tc * 2 + half) % 3
                waits = [xT_done, w_tok]
                if pq_free[k] is not None:
                    waits.append(pq_free[k])
                for dc in range(NDC):
                    ev("tensor", lambda e, k=k, dc=dc, tc=tc, half=half:
                       e.matmul(psq[k][:, :],
                                xTbf[:, dc * 2048 + tc * 128: dc * 2048 + (tc + 1) * 128],
                                wvb[:, dc * 1024 + half * 512: dc * 1024 + (half + 1) * 512],
                                start=(dc == 0), stop=(dc == NDC - 1)),
                       waits=(waits if dc == 0 else ()),
                       inc=("pe" if dc == NDC - 1 else None))
                mm_t = tok("pe")
                ev("vector", lambda e, k=k, tc=tc, half=half:
                   e.tensor_copy(
                       Vhat[:, tc * 1040 + half * 520: tc * 1040 + (half + 1) * 520]
                       .rearrange("p (h e) -> p h e", h=8)[:, :, 0:64],
                       psq[k][:, :].rearrange("p (h e) -> p h e", h=8)),
                   waits=[mm_t, ones_t], inc="ve")
                pq_free[k] = tok("ve")
        qkv_done_ve = tok("ve")

        p.serialize = ("p1" in KSER) or ("p1e" in KSER)
        def norms(src, dstn, rdy):
            sq_free = [None, None]
            prev_ln = None
            for tt in range(4):
                for hp in range(NHC):
                    k = hp % 2
                    waits = [rdy]
                    if sq_free[k] is not None:
                        waits.append(sq_free[k])
                    ev("vector", lambda e, k=k, hp=hp, tt=tt, src=src:
                       e.tensor_tensor(qsq[k][:, :],
                                       src[:, hp * 2048 + tt * 512: hp * 2048 + (tt + 1) * 512],
                                       src[:, hp * 2048 + tt * 512: hp * 2048 + (tt + 1) * 512],
                                       Alu.mult),
                       waits=waits, inc="ve")
                    sq_t = tok("ve")
                    w_mm = [sq_t]
                    if hp == 0 and prev_ln is not None:
                        w_mm.append(prev_ln)
                    ev("tensor", lambda e, k=k, hp=hp:
                       e.matmul(pnorm[:, :],
                                hsel[:, hp * 16:(hp + 1) * 16], qsq[k][:, :],
                                start=(hp == 0), stop=(hp == NHC - 1)),
                       waits=w_mm, inc="pe")
                    sq_free[k] = tok("pe")
                nm_t = tok("pe")
                ev("scalar", lambda e, dstn=dstn, tt=tt: e.activation(
                    dstn[:, tt * 512:(tt + 1) * 512], pnorm[:, :], AF.Ln,
                    bias=fc[0:16, 0:1], scale=1.0),
                   waits=[nm_t], inc="se")
                ev("scalar", lambda e, dstn=dstn, tt=tt: e.activation(
                    dstn[:, tt * 512:(tt + 1) * 512],
                    dstn[:, tt * 512:(tt + 1) * 512], AF.Exp,
                    bias=0.0, scale=fc[0:16, 2:3]), inc="se")
                prev_ln = tok("se")
            return prev_ln

        qn_t = norms(Qt, qnorm, q_done_ve)

        sq_free = [None, None]
        prev_ln = qn_t
        for tt in range(4):
            for hp in range(NHC):
                k = hp % 2
                waits = [k_done_ve]
                if sq_free[k] is not None:
                    waits.append(sq_free[k])
                else:
                    waits.append(qn_t)
                ev("vector", lambda e, k=k, hp=hp, tt=tt:
                   e.tensor_tensor(qsq[k][:, :],
                                   Kt[:, hp * 2048 + tt * 512: hp * 2048 + (tt + 1) * 512],
                                   Kt[:, hp * 2048 + tt * 512: hp * 2048 + (tt + 1) * 512],
                                   Alu.mult),
                   waits=waits, inc="ve")
                sq_t = tok("ve")
                w_mm = [sq_t]
                if hp == 0:
                    w_mm.append(prev_ln)
                ev("tensor", lambda e, k=k, hp=hp:
                   e.matmul(pnorm[:, :], hsel[:, hp * 16:(hp + 1) * 16], qsq[k][:, :],
                            start=(hp == 0), stop=(hp == NHC - 1)),
                   waits=w_mm, inc="pe")
                sq_free[k] = tok("pe")
            nm_t = tok("pe")
            kred_prev = tok("ve")
            ev("scalar", lambda e, tt=tt: e.activation(
                ktmp[:, :], pnorm[:, :], AF.Ln, bias=fc[0:16, 0:1], scale=1.0),
               waits=[nm_t, kred_prev], inc="se")
            ev("scalar", lambda e: e.activation(
                ktmp[:, :], ktmp[:, :], AF.Exp, bias=0.0, scale=fc[0:16, 2:3]), inc="se")
            prev_ln = tok("se")
            ev("vector", lambda e, tt=tt: e.tensor_reduce(
                ssc[0:16, 16 + tt:17 + tt], ktmp[:, :], axis=AX, op=Alu.max),
               waits=[prev_ln], inc="ve")
        kr_t = tok("ve")
        ev("vector", lambda e: e.tensor_reduce(ssc[0:16, 0:1], ssc[0:16, 16:20],
                                               axis=AX, op=Alu.max),
           waits=[kr_t], inc="ve")
        _t = tok("ve")
        ev("vector", lambda e: e.tensor_scalar(ssc[0:16, 1:2], ssc[0:16, 0:1],
                                               -1.0, None, Alu.mult),
           waits=[_t], inc="ve")
        kx_t = tok("ve")
        ev("vector", lambda e: e.tensor_scalar(qnorm[:, :], qnorm[:, :],
                                               ssc[0:16, 1:2], None, Alu.mult),
           waits=[qn_t, kx_t], inc="ve")
        for qt in range(4):
            ev("vector", lambda e, qt=qt: e.tensor_reduce(
                ssc[0:16, 2 + qt:3 + qt], qnorm[:, qt * 512:(qt + 1) * 512],
                axis=AX, op=Alu.min), inc="ve")
        red_t = tok("ve")
        ev("sync", lambda e: e.dma_start(
            out=bounce.ap().rearrange("o (a b) -> (o a) b", a=16),
            in_=ssc[0:16, 2:6]), waits=[red_t], inc="ld", n=16)
        b1 = tok("ld")
        ev("sync", lambda e: e.dma_start(out=sflat[:, :], in_=bounce[:, :]),
           waits=[b1], inc="ld", n=16)
        b2 = tok("ld")
        ev("vector", lambda e: e.tensor_tensor(
            srow[0:1, 0:32].rearrange("o (p q) -> o p q", p=8),
            sflat[0:1, :].rearrange("o (p q) -> o p q", p=8)[:, :, 0:4],
            sflat[0:1, :].rearrange("o (p q) -> o p q", p=8)[:, :, 4:8],
            Alu.min), waits=[b2], inc="ve")
        pm_t = tok("ve")
        ev("tensor", lambda e: e.matmul(pbc[:, 0:32], onesf[0:1, :], srow[0:1, 0:32],
                                        start=True, stop=True),
           waits=[pm_t], inc="pe")
        bc_t = tok("pe")
        ev("vector", lambda e: e.tensor_copy(negc_b[:, :], pbc[:, 0:32]),
           waits=[bc_t], inc="ve")

        if "qnorm" in dbg:
            ev("sync", lambda e: e.dma_start(out=dbg["qnorm"][s], in_=qnorm[:, :]),
               waits=[tok("ve")], inc="ld", n=16)
            ev("sync", lambda e: e.dma_start(out=dbg["ssc"][s], in_=ssc[:, :]),
               inc="ld", n=16)
            ev("sync", lambda e: e.dma_start(out=dbg["sflat"][s], in_=sflat[:, :]),
               inc="ld", n=16)
            ev("sync", lambda e: e.dma_start(out=dbg["srow32"][s], in_=srow[0:1, 0:32]),
               inc="ld", n=16)
        if "qt" in dbg:
            ev("gpsimd", lambda e: e.dma_start(out=dbg["qt"][s], in_=Qt[:, :]),
               waits=[tok("ve")], inc="gld", n=16)
            ev("gpsimd", lambda e: e.dma_start(out=dbg["kt"][s], in_=Kt[:, :]),
               inc="gld", n=16)
            ev("gpsimd", lambda e: e.dma_start(out=dbg["vh"][s], in_=Vhat[:, :]),
               inc="gld", n=16)
            ev("gpsimd", lambda e: e.dma_start(out=dbg["negc"][s], in_=negc_b[:, :]),
               inc="gld", n=16)

        p.barrier()
        p.flush()
        ph.close()

        ph = ExitStack()
        OT = sbt("OT", [128, 16384], BF16, seq_es, side="right")
        ab = [sbt(f"ab_{i}", [128, 1024], BF16, ph) for i in range(4)]
        osb = [sbt(f"osb_{i}", [128, 1024], BF16, ph) for i in range(2)]
        rcb = [sbt(f"rcb_{i}", [64, 1024], F32, ph) for i in range(2)]
        erow = sbt("erow", [128, 4096], F32, ph)
        psS = [pst(f"psS_{i}", [128, 1024], F32, ph) for i in range(2)]
        psO = [pst(f"psO_{i}", [128, 1024], F32, ph) for i in range(2)]

        p.serialize = "p2" in KSER
        exp_done = {}
        av_done = {}
        psO_free = {}
        pending = []

        tt_done = {}

        def emit_pending():
            for ctx, hp0, qt0 in pending:
                rcs = f"rc{ctx}"
                w_tt = [(rcs, cnt[rcs])]
                for h in range(2):
                    ev("vector", lambda e, h=h, hp0=hp0, qt0=qt0, ctx=ctx:
                       e.tensor_tensor(
                        OT[h * 64:(h + 1) * 64,
                           hp0 * 2048 + qt0 * 512: hp0 * 2048 + (qt0 + 1) * 512],
                        osb[ctx][0:64, h * 512:(h + 1) * 512],
                        rcb[ctx][:, h * 512:(h + 1) * 512], Alu.mult),
                       waits=(w_tt if h == 0 else ()), inc="ve")
                tt_done[ctx] = tok("ve")
            pending.clear()

        for hpp in range(4):
            for qt in range(4):
                nkc = 4 * qt + 4
                av_args = {}

                def emit_av(ctx, kc):
                    (ai, hp, w_av, col0, span) = av_args.pop(ctx)
                    for h in range(2):
                        ev("tensor", lambda e, ctx=ctx, ai=ai, hp=hp, kc=kc, h=h,
                           col0=col0, span=span, kcf=(kc == 0):
                           e.matmul(psO[ctx][0:65, h * 512 + col0: h * 512 + col0 + span],
                                    Vhat[:, kc * 1040 + (2 * hp + h) * 65: kc * 1040 + (2 * hp + h) * 65 + 65],
                                    ab[ai][:, h * 512: h * 512 + span],
                                    start=kcf, stop=(kc == nkc - 1)),
                           waits=(w_av if h == 0 else ()),
                           inc=("pe" if h == 1 else None))
                    av_done[(ctx, kc % 2)] = tok("pe")

                for kc in range(nkc):
                    if kc == 2:
                        emit_pending()
                    qs = max(qt * 512, kc * 128)
                    span = (qt + 1) * 512 - qs
                    diag = kc * 128 >= qt * 512
                    s_toks = {}
                    for ctx in range(2):
                        hp = 2 * hpp + ctx
                        w_s = []
                        if ctx in exp_done:
                            w_s.append(exp_done[ctx])
                        ev("tensor", lambda e, ctx=ctx, hp=hp, kc=kc, qs=qs, span=span:
                           e.matmul(psS[ctx][:, 0:span],
                                    Kt[0:64, hp * 2048 + kc * 128: hp * 2048 + (kc + 1) * 128],
                                    Qt[0:64, hp * 2048 + qs: hp * 2048 + qs + span],
                                    start=True, stop=True, tile_position=(0, 0)),
                           waits=w_s, inc=None)
                        ev("tensor", lambda e, ctx=ctx, hp=hp, kc=kc, qs=qs, span=span:
                           e.matmul(psS[ctx][:, 512:512 + span],
                                    Kt[64:128, hp * 2048 + kc * 128: hp * 2048 + (kc + 1) * 128],
                                    Qt[64:128, hp * 2048 + qs: hp * 2048 + qs + span],
                                    start=True, stop=True, tile_position=(64, 0)),
                           inc="pe")
                        s_toks[ctx] = tok("pe")
                    for ctx in range(2):
                        if ctx in av_args:
                            emit_av(ctx, kc - 1)
                    for ctx in range(2):
                        hp = 2 * hpp + ctx
                        ai = ctx * 2 + kc % 2
                        w_e = [s_toks[ctx]]
                        if (ctx, kc % 2) in av_done:
                            w_e.append(av_done[(ctx, kc % 2)])
                        ev("scalar", lambda e, ctx=ctx, ai=ai, hp=hp, qt=qt, span=span:
                           e.activation(
                               ab[ai][:, :].rearrange("p (h q) -> p h q", h=2)[:, :, 0:span],
                               psS[ctx][:, :].rearrange("p (h q) -> p h q", h=2)[:, :, 0:span],
                               AF.Exp, bias=negc_b[:, hp * 4 + qt: hp * 4 + qt + 1],
                               scale=1.0),
                           waits=w_e, inc="se")
                        exp_done[ctx] = tok("se")
                        w_av = [exp_done[ctx]]
                        if diag:
                            for h in range(2):
                                ev("vector", lambda e, ai=ai, h=h:
                                   e.tensor_tensor(ab[ai][:, h * 512: h * 512 + 128],
                                                   ab[ai][:, h * 512: h * 512 + 128],
                                                   maskb[:, :], Alu.mult),
                                   waits=([exp_done[ctx]] if h == 0 else ()), inc="ve")
                            w_av.append(tok("ve"))
                        if kc == 0 and ctx in psO_free:
                            w_av.extend(psO_free[ctx])
                        col0 = qs - qt * 512
                        av_args[ctx] = (ai, 2 * hpp + ctx, w_av, col0, span)
                for ctx in range(2):
                    emit_av(ctx, nkc - 1)
                for ctx in range(2):
                    hp = 2 * hpp + ctx
                    rcs = f"rc{ctx}"
                    last_av = av_done[(ctx, (nkc - 1) % 2)]
                    ev("vector", lambda e, ctx=ctx: e.tensor_copy(
                        osb[ctx][0:64, :], psO[ctx][0:64, 0:1024]),
                       waits=[last_av], inc="ve")
                    ev("vector", lambda e, ctx=ctx: e.tensor_copy(
                        erow[64:65, ctx * 1024:(ctx + 1) * 1024],
                        psO[ctx][64:65, 0:1024]),
                       waits=[(rcs, cnt[rcs])], inc="ve")
                    ocp_t = tok("ve")
                    psO_free[ctx] = (ocp_t,)
                    ev("sync", lambda e, ctx=ctx: e.dma_start(
                        out=rc_dram[ctx, 0], in_=erow[64:65, ctx * 1024:(ctx + 1) * 1024]),
                       waits=[ocp_t], inc=rcs, n=16)
                    ev("sync", lambda e, ctx=ctx: e.dma_start(
                        out=erow[0:1, ctx * 1024:(ctx + 1) * 1024],
                        in_=rc_dram[ctx, 0][None, :] if False else rc_dram[ctx, 0:1, :].rearrange("a b -> a b")),
                       waits=[(rcs, cnt[rcs])], inc=rcs, n=16)
                    ld2 = (rcs, cnt[rcs])
                    ev("scalar", lambda e, ctx=ctx: e.activation(
                        erow[0:1, 2048 + ctx * 1024: 2048 + (ctx + 1) * 1024],
                        erow[0:1, ctx * 1024:(ctx + 1) * 1024], AF.Ln,
                        bias=0.0, scale=1.0),
                       waits=[ld2], inc="se")
                    ev("scalar", lambda e, ctx=ctx: e.activation(
                        erow[0:1, 2048 + ctx * 1024: 2048 + (ctx + 1) * 1024],
                        erow[0:1, 2048 + ctx * 1024: 2048 + (ctx + 1) * 1024], AF.Exp,
                        bias=0.0, scale=fc[0:1, 4:5]), inc="se")
                    rcp_t = tok("se")
                    ev("sync", lambda e, ctx=ctx: e.dma_start(
                        out=rc_dram[ctx, 1], in_=erow[0:1, 2048 + ctx * 1024: 2048 + (ctx + 1) * 1024]),
                       waits=[rcp_t], inc=rcs, n=16)
                    w_bc = [(rcs, cnt[rcs])]
                    if ctx in tt_done:
                        w_bc.append(tt_done[ctx])
                    ev("sync", lambda e, ctx=ctx: e.dma_start(
                        out=rcb[ctx][:, :],
                        in_=bass.AP(rc_dram, ctx * 2048 + 1024, [[0, 64], [1, 1024]])),
                       waits=w_bc, inc=rcs, n=16)
                    pending.append((ctx, hp, qt))
        emit_pending()

        if "ot" in dbg:
            ev("gpsimd", lambda e: e.dma_start(out=dbg["ot"][s], in_=OT[:, :]),
               waits=[tok("ve")], inc="gld", n=16)

        p.barrier()
        p.flush()
        ph.close()
        att_es.close()

        ph = ExitStack()
        xT32 = sbt("xT32", [128, 16384], F32, seq_es, side="right")
        wob = sbt("wob", [128, 8192], BF16, ph)
        rb = sbt("rb", [128, 4096], BF16, ph)
        rsq = [sbt(f"rsq_{i}", [128, 512], BF16, ph) for i in range(3)]
        xs32b = [sbt(f"xs32b_{i}", [128, 512], F32, ph) for i in range(2)]
        pr = [pst(f"pr_{i}", [128, 512], F32, ph) for i in range(3)]
        ptr32 = pst("ptr32", [128, 512], F32, ph)
        pm = pst("pm", [1, 512], F32, ph)
        ps2 = pst("ps2", [1, 512], F32, ph)
        pbc_nm = pst("pbc_nm", [128, 512], F32, ph)
        pbc_rs = pst("pbc_rs", [128, 512], F32, ph)

        for hp in range(NHC):
            ev("gpsimd", lambda e, hp=hp: e.dma_start(
                out=wob[:, hp * 1024:(hp + 1) * 1024],
                in_=wo_in[hp * 128:(hp + 1) * 128, :]), inc="gld", n=16)
        wo_tok = tok("gld")

        p.serialize = "p3" in KSER
        xr_state = {}

        def emit_xreload(tc):
            for g in range(2):
                idx = 2 * tc + g
                bi = idx % 2
                waits = []
                if ("evac", idx - 2) in xr_state:
                    waits.append(xr_state[("evac", idx - 2)])
                lxs = "lxa" if idx % 2 == 0 else "lxb"
                ev("sync", lambda e, bi=bi, tc=tc, g=g:
                   e.dma_start(out=xs32b[bi][:, :],
                               in_=x_in[s * L + tc * 128: s * L + (tc + 1) * 128,
                                        g * 512:(g + 1) * 512]),
                   waits=waits, inc=lxs, n=16)
                ld_t = tok(lxs)
                wv_ = [ld_t]
                if ("evac", idx - 1) in xr_state:
                    wv_.append(xr_state[("evac", idx - 1)])
                for j in range(4):
                    ev("tensor", lambda e, bi=bi, j=j:
                       e.transpose(ptr32[:, j * 128:(j + 1) * 128],
                                   xs32b[bi][:, j * 128:(j + 1) * 128], identf[:, :]),
                       waits=(wv_ if j == 0 else ()))
                ev("tensor", lambda e: e.drain(), inc="pe")
                tr_t = tok("pe")
                ev("vector", lambda e, tc=tc, g=g: e.tensor_copy(
                    xT32[:, :].rearrange("p (c t) -> p c t", c=8)[:, g * 4:(g + 1) * 4,
                                                                 tc * 128:(tc + 1) * 128],
                    ptr32[:, :].rearrange("p (j t) -> p j t", j=4)),
                   waits=[tr_t], inc="ve")
                xr_state[("evac", idx)] = tok("ve")

        pr_free = [None, None, None]
        bc_free = None
        stats_pend = []

        def flush_stats(bcf):
            for (dcx, ct2, sqt) in stats_pend:
                w_m1 = [ct2]
                if dcx == 0 and bcf is not None:
                    w_m1.append(bcf)
                ev("tensor", lambda e, dcx=dcx: e.matmul(
                    pm[:, :], onesb[:, 0:1], rb[:, dcx * 512:(dcx + 1) * 512],
                    start=(dcx == 0), stop=(dcx == NDC - 1)), waits=w_m1, inc=None)
                ev("tensor", lambda e, dcx=dcx: e.matmul(
                    ps2[:, :], onesb[:, 0:1], rsq[dcx % 3][:, :],
                    start=(dcx == 0), stop=(dcx == NDC - 1)), waits=[sqt], inc="pe")
            stats_pend.clear()

        for tt in range(4):
            p.serialize = ("p3" in KSER) or ("p3a" in KSER)
            for tc in range(4 * tt, 4 * tt + 4):
                emit_xreload(tc)
            p.serialize = ("p3" in KSER) or ("p3b" in KSER)
            xr_t = tok("ve")
            r_toks = []
            for dc in range(NDC):
                k = (tt * NDC + dc) % 3
                waits = [wo_tok]
                if pr_free[k] is not None:
                    waits.append(pr_free[k])
                for hp in range(NHC):
                    ev("tensor", lambda e, k=k, hp=hp, dc=dc, tt=tt:
                       e.matmul(pr[k][:, :],
                                wob[:, hp * 1024 + dc * 128: hp * 1024 + (dc + 1) * 128],
                                OT[:, hp * 2048 + tt * 512: hp * 2048 + (tt + 1) * 512],
                                start=(hp == 0), stop=(hp == NHC - 1)),
                       waits=(waits if hp == 0 else ()),
                       inc=("pe" if hp == NHC - 1 else None))
                mm_t = tok("pe")
                flush_stats(bc_free)
                ev("vector", lambda e, k=k, dc=dc, tt=tt, _w=xr_t: e.scalar_tensor_tensor(
                    xT32[:, dc * 2048 + tt * 512: dc * 2048 + (tt + 1) * 512],
                    pr[k][:, :], 1.0,
                    xT32[:, dc * 2048 + tt * 512: dc * 2048 + (tt + 1) * 512],
                    Alu.mult, Alu.add),
                   waits=[mm_t, xr_t], inc="ve")
                pr_free[k] = tok("ve")
                r_toks.append(tok("ve"))
                r_t = tok("ve")
                ev("scalar", lambda e, dc=dc, tt=tt: e.copy(
                    rb[:, dc * 512:(dc + 1) * 512],
                    xT32[:, dc * 2048 + tt * 512: dc * 2048 + (tt + 1) * 512]),
                   waits=[r_t], inc="se")
                cast_t2 = tok("se")
                ev("gpsimd", lambda e, dc=dc: e.tensor_tensor(
                    rsq[dc % 3][:, :], rb[:, dc * 512:(dc + 1) * 512],
                    rb[:, dc * 512:(dc + 1) * 512], Alu.mult),
                   waits=([cast_t2] if dc < 3 else [cast_t2, tok("pe")]), inc="ge")
                sq_t = tok("ge")
                stats_pend.append((dc, cast_t2, sq_t))
            flush_stats(bc_free)
            st_mm = tok("pe")
            ev("vector", lambda e: e.tensor_scalar(srow[0:1, 0:512], pm[:, :],
                                                   1.0 / D, None, Alu.mult),
               waits=[st_mm], inc="ve")
            ev("vector", lambda e: e.tensor_tensor(srow[0:1, 512:1024],
                                                   srow[0:1, 0:512], srow[0:1, 0:512],
                                                   Alu.mult), inc="ve")
            ev("vector", lambda e: e.scalar_tensor_tensor(
                srow[0:1, 1024:1536], ps2[:, :], 1.0 / D, srow[0:1, 512:1024],
                Alu.mult, Alu.subtract), inc="ve")
            ev("vector", lambda e: e.tensor_scalar(srow[0:1, 1536:2048],
                                                   srow[0:1, 0:512], -1.0, None,
                                                   Alu.mult), inc="ve")
            var_t = tok("ve")
            ev("scalar", lambda e: e.activation(srow[0:1, 1024:1536],
                                                srow[0:1, 1024:1536], AF.Ln,
                                                bias=fc[0:1, 1:2], scale=1.0),
               waits=[var_t], inc="se")
            ev("scalar", lambda e: e.activation(srow[0:1, 1024:1536],
                                                srow[0:1, 1024:1536], AF.Exp,
                                                bias=0.0, scale=fc[0:1, 3:4]), inc="se")
            rstd_t = tok("se")
            ev("tensor", lambda e: e.matmul(pbc_nm[:, :], onesf[0:1, :],
                                            srow[0:1, 1536:2048], start=True, stop=True),
               waits=[rstd_t], inc="pe")
            ev("tensor", lambda e: e.matmul(pbc_rs[:, :], onesf[0:1, :],
                                            srow[0:1, 1024:1536], start=True, stop=True),
               inc="pe")
            bct = tok("pe")
            for dc in range(NDC):
                ev("vector", lambda e, dc=dc, tt=tt: e.tensor_tensor(
                    xT32[:, dc * 2048 + tt * 512: dc * 2048 + (tt + 1) * 512],
                    xT32[:, dc * 2048 + tt * 512: dc * 2048 + (tt + 1) * 512],
                    pbc_nm[:, :], Alu.add),
                   waits=[bct] if dc == 0 else (), inc=None)
                ev("vector", lambda e, dc=dc, tt=tt: e.tensor_tensor(
                    xT32[:, dc * 2048 + tt * 512: dc * 2048 + (tt + 1) * 512],
                    xT32[:, dc * 2048 + tt * 512: dc * 2048 + (tt + 1) * 512],
                    pbc_rs[:, :], Alu.mult), inc="ve")
                x1_t = tok("ve")
                ev("vector", lambda e, dc=dc, tt=tt, s=s: e.tensor_reduce(
                    amax_buf[:, s * 32 + tt * 8 + dc: s * 32 + tt * 8 + dc + 1],
                    xT32[:, dc * 2048 + tt * 512: dc * 2048 + (tt + 1) * 512],
                    axis=AX, op=Alu.max, apply_absolute_value=True), inc="ve")
                ev("sync", lambda e, dc=dc, tt=tt, s=s: e.dma_start(
                    out=x1d[dc, :, s * L + tt * 512: s * L + (tt + 1) * 512],
                    in_=xT32[:, dc * 2048 + tt * 512: dc * 2048 + (tt + 1) * 512]),
                   waits=[x1_t], inc="ld", n=16)
            bc_free = tok("ve")

        if "x1" in dbg:
            ev("sync", lambda e: e.dma_start(out=dbg["x1"][s], in_=xT32[:, :]),
               waits=[tok("ve")], inc="ld", n=16)

        p.barrier()
        p.flush()
        ph.close()
        seq_es.close()

    for s in range(NSEQ):
        attention_seq(s)

    ph = ExitStack()
    pfold = pst("pfold", [128, 512], F32, ph)

    def fold_and_scales(buf, ncols, cc_i, cc_o, sagc, clip_col, invb):
        ev("vector", lambda e: e.tensor_reduce(ssc[:, 8:9], buf[:, 0:ncols], axis=AX,
                                               op=Alu.max, apply_absolute_value=True),
           inc="ve")
        f1 = tok("ve")
        ev("tensor", lambda e: e.transpose(pfold[0:1, 0:128], ssc[:, 8:9], identf[:, :]),
           waits=[f1])
        ev("tensor", lambda e: e.drain(), inc="pe")
        f2 = tok("pe")
        ev("vector", lambda e: e.tensor_reduce(srow[0:1, 0:1], pfold[0:1, 0:128],
                                               axis=AX, op=Alu.max,
                                               apply_absolute_value=True),
           waits=[f2], inc="ve")
        f3 = tok("ve")
        ev("sync", lambda e, cc_i=cc_i: e.dma_start(out=cc_i[0:1, 0:1],
                                                    in_=srow[0:1, 0:1]),
           waits=[f3], inc="ld", n=16)
        f4 = tok("ld")
        ev("gpsimd", lambda e, cc_i=cc_i, cc_o=cc_o: e.collective_compute(
            "AllGather", Alu.bypass, replica_groups=[list(range(NCORE))],
            ins=[cc_i.ap().opt()], outs=[cc_o.ap().opt()]),
           waits=[f4], inc="cc")
        f5 = tok("cc")
        ev("gpsimd", lambda e, cc_o=cc_o, sagc=sagc: e.dma_start(
            out=srow[0:1, sagc:sagc + 128], in_=cc_o[:, :]),
           waits=[f5], inc="gld", n=16)
        f6 = tok("gld")
        ev("vector", lambda e, sagc=sagc: e.tensor_reduce(
            srow[0:1, 1:2],
            srow[0:1, sagc:sagc + 128].rearrange("o (r j) -> o r j", r=8)[:, :, 0:1]
            .rearrange("o r j -> o (r j)"),
            axis=AX, op=Alu.max, apply_absolute_value=True),
           waits=[f6], inc="ve")
        gmax_t = tok("ve")
        ev("scalar", lambda e, clip_col=clip_col: e.activation(
            srow[0:1, 2:3], clips_sb[0:1, clip_col:clip_col + 1], AF.Exp,
            bias=0.0, scale=fc[0:1, 4:5]), inc="se")
        sg_t = tok("se")
        ev("vector", lambda e: e.tensor_scalar(srow[0:1, 2:3], srow[0:1, 2:3],
                                               1.0, None, Alu.add),
           waits=[sg_t], inc="ve")
        _pt = tok("ve")
        ev("vector", lambda e: e.reciprocal(srow[0:1, 2:3], srow[0:1, 2:3]),
           waits=[_pt], inc="ve")
        _rt = tok("ve")
        ev("vector", lambda e: e.tensor_tensor(srow[0:1, 3:4], srow[0:1, 2:3],
                                               srow[0:1, 1:2], Alu.mult),
           waits=[_rt, gmax_t], inc="ve")
        _tt = tok("ve")
        ev("vector", lambda e: e.tensor_scalar(srow[0:1, 4:5], srow[0:1, 3:4],
                                               1.0 / 127.0, None, Alu.mult),
           waits=[_tt], inc="ve")
        ev("vector", lambda e: e.reciprocal(srow[0:1, 5:6], srow[0:1, 3:4]), inc="ve")
        _rt2 = tok("ve")
        ev("vector", lambda e: e.tensor_scalar(srow[0:1, 5:6], srow[0:1, 5:6],
                                               127.0, None, Alu.mult),
           waits=[_rt2], inc="ve")
        sc_t = tok("ve")
        ev("tensor", lambda e: e.matmul(pfold[:, 0:2], onesf[0:1, :], srow[0:1, 4:6],
                                        start=True, stop=True),
           waits=[sc_t], inc="pe")
        bc2 = tok("pe")
        ev("vector", lambda e, invb=invb: e.tensor_copy(invb[:, :], pfold[:, 0:2]),
           waits=[bc2], inc="ve")

    fold_and_scales(amax_buf, 64, cc1_in, cc1_out, 128, 0, xinvb)
    p.barrier()
    p.flush()
    ph.close()

    ph = ExitStack()
    xint = sbt("xint", [128, 32768], BF16, ph)
    x1s = [sbt(f"x1s_{i}", [128, 4096], F32, ph) for i in range(2)]
    i32t = sbt("i32t", [128, 4096], I32, ph)
    wgs = [sbt(f"wgs_{i}", [128, 1024], F32, ph) for i in range(2)]
    wus = [sbt(f"wus_{i}", [128, 1024], F32, ph) for i in range(2)]
    wi32 = sbt("wi32", [128, 1024], I32, ph)
    wintb = sbt("wintb", [128, 2048], BF16, ph)
    wgT = [sbt(f"wgT_{i}", [128, 1024], BF16, ph) for i in range(2)]
    wuT = [sbt(f"wuT_{i}", [128, 1024], BF16, ph) for i in range(2)]
    vstage = [sbt(f"vstage_{i}", [128, 4096], F32, ph) for i in range(2)]
    silu_t = [sbt(f"silu_{i}", [128, 512], F32, ph) for i in range(2)]
    vm8 = sbt("vm8", [128, 8], F32, ph)
    pgu = [pst(f"pgu_{i}", [128, 512], F32, ph) for i in range(4)]
    ptw = pst("ptw", [128, 1024], BF16, ph)

    p.serialize = "p5" in KSER
    q_t = {}
    for dc in range(NDC):
        bi = dc % 2
        waits = []
        if dc >= 2:
            waits.append(q_t[dc - 2])
        lxs = "lxa" if dc % 2 == 0 else "lxb"
        ev("sync", lambda e, bi=bi, dc=dc: e.dma_start(out=x1s[bi][:, :],
                                                       in_=x1d[dc, :, :]),
           waits=waits, inc=lxs, n=16)
        ld_t = tok(lxs)
        w2 = [ld_t]
        if dc >= 1:
            w2.append(q_t[dc - 1])
        ev("vector", lambda e, bi=bi: e.tensor_scalar(i32t[:, :], x1s[bi][:, :],
                                                      xinvb[:, 1:2], None, Alu.mult),
           waits=w2, inc="ve")
        ev("vector", lambda e, dc=dc: e.tensor_scalar(
            xint[:, dc * 4096:(dc + 1) * 4096], i32t[:, :], 127.0, -128.0,
            Alu.min, Alu.max), inc="ve")
        q_t[dc] = tok("ve")
    xint_done = tok("ve")

    ev("sync", lambda e: e.dma_start(out=wgs[0][:, :], in_=wg_in[0:128, :]),
       inc="lwa", n=16)
    ev("sync", lambda e: e.dma_start(out=wus[0][:, :], in_=wu_in[0:128, :]),
       inc="lwa", n=16)
    wld_t = {0: tok("lwa")}

    pg_free = [None, None]
    spill_t = {}
    tq_free = {}
    silu_done = {}
    for ic in range(NIC):
        bi = ic % 2
        scg_col = 12 + 3 * (ic % 2)
        if ic + 1 < NIC:
            waits = []
            if (ic - 1) in tq_free:
                waits.append(tq_free[ic - 1])
            lws = "lwa" if (ic + 1) % 2 == 0 else "lwb"
            ev("sync", lambda e, ic=ic, b2=(ic + 1) % 2: e.dma_start(
                out=wgs[b2][:, :], in_=wg_in[(ic + 1) * 128:(ic + 2) * 128, :]),
               waits=waits, inc=lws, n=16)
            ev("sync", lambda e, ic=ic, b2=(ic + 1) % 2: e.dma_start(
                out=wus[b2][:, :], in_=wu_in[(ic + 1) * 128:(ic + 2) * 128, :]),
               inc=lws, n=16)
            wld_t[ic + 1] = tok(lws)

        prev_tp = tok("pe")
        w_q0 = [wld_t[ic], prev_tp]
        if (ic - 2) in silu_done:
            w_q0.append(silu_done[ic - 2])
        for wi, (wstage, col0) in enumerate(((wgs[bi], 0), (wus[bi], 1024))):
            ev("vector", lambda e, wstage=wstage: e.tensor_reduce(
                ssc[:, 9:10], wstage[:, :], axis=AX, op=Alu.max,
                apply_absolute_value=True),
               waits=(w_q0 if wi == 0 else [wld_t[ic]]), inc="ve")
            _t = tok("ve")
            ev("vector", lambda e: e.reciprocal(ssc[:, 10:11], ssc[:, 9:10]),
               waits=[_t], inc="ve")
            _t = tok("ve")
            ev("vector", lambda e: e.tensor_scalar(ssc[:, 10:11], ssc[:, 10:11],
                                                   127.0, None, Alu.mult),
               waits=[_t], inc="ve")
            _t2 = tok("ve")
            ev("vector", lambda e, wstage=wstage: e.tensor_scalar(
                wi32[:, :], wstage[:, :], ssc[:, 10:11], None, Alu.mult),
               waits=[_t2], inc="ve")
            ev("vector", lambda e, col0=col0: e.tensor_scalar(
                wintb[:, col0:col0 + 1024], wi32[:, :], 127.0, -128.0,
                Alu.min, Alu.max), inc="ve")
            if wi == 0:
                ev("vector", lambda e: e.tensor_scalar(ssc[:, 11:12], ssc[:, 9:10],
                                                       1.0 / 127.0, None, Alu.mult),
                   waits=[_t], inc="ve")
                _t3 = tok("ve")
                ev("vector", lambda e, scg_col=scg_col: e.tensor_scalar(
                    ssc[:, scg_col:scg_col + 1], ssc[:, 11:12],
                    xinvb[:, 0:1], None, Alu.mult),
                   waits=[_t3], inc="ve")
            else:
                ev("vector", lambda e: e.tensor_scalar(ssc[:, 13:14], ssc[:, 9:10],
                                                       1.0 / 127.0, None, Alu.mult),
                   waits=[_t], inc="ve")
                _t3 = tok("ve")
                ev("vector", lambda e, ic=ic: e.tensor_scalar(
                    wsu_sc[:, ic:ic + 1], ssc[:, 13:14], xinvb[:, 0:1], None,
                    Alu.mult), waits=[_t3], inc="ve")
        tq_free[ic] = tok("ve")
        q_done = tok("ve")

        prev_evac = None
        for wi, (col0, dstT) in enumerate(((0, wgT[bi]), (1024, wuT[bi]))):
            w_tp = [q_done] if wi == 0 else [prev_evac]
            for hc in range(NHC):
                ev("tensor", lambda e, col0=col0, hc=hc: e.transpose(
                    ptw[:, hc * 128:(hc + 1) * 128],
                    wintb[:, col0 + hc * 128: col0 + (hc + 1) * 128], identb[:, :]),
                   waits=(w_tp if hc == 0 else ()))
            ev("tensor", lambda e: e.drain(), inc="pe")
            tp_t = tok("pe")
            ev("vector", lambda e, dstT=dstT: e.tensor_copy(dstT[:, :], ptw[:, :]),
               waits=[tp_t], inc="ve")
            prev_evac = tok("ve")
        wT_t = tok("ve")

        vred = []
        for tt in range(NTT):
            kg, ku = 2 * (tt % 2), 2 * (tt % 2) + 1
            w_mm = [wT_t, xint_done]
            if pg_free[tt % 2] is not None:
                w_mm.append(pg_free[tt % 2])
            if ic >= 2 and (ic - 2) in spill_t:
                w_mm.append(spill_t[ic - 2])
            for hc in range(NHC):
                ev("tensor", lambda e, kg=kg, bi=bi, hc=hc, tt=tt: e.matmul(
                    pgu[kg][:, :], wgT[bi][:, hc * 128:(hc + 1) * 128],
                    xint[:, hc * 4096 + tt * 512: hc * 4096 + (tt + 1) * 512],
                    start=(hc == 0), stop=(hc == NHC - 1)),
                   waits=(w_mm if hc == 0 else ()), inc=None)
            for hc in range(NHC):
                ev("tensor", lambda e, ku=ku, bi=bi, hc=hc, tt=tt: e.matmul(
                    pgu[ku][:, :], wuT[bi][:, hc * 128:(hc + 1) * 128],
                    xint[:, hc * 4096 + tt * 512: hc * 4096 + (tt + 1) * 512],
                    start=(hc == 0), stop=(hc == NHC - 1)),
                   inc=("pe" if hc == NHC - 1 else None))
            mm_t = tok("pe")
            ev("scalar", lambda e, kg=kg, tt=tt, scg_col=scg_col: e.activation(
                silu_t[tt % 2][:, :], pgu[kg][:, :], AF.Silu,
                bias=0.0, scale=ssc[:, scg_col:scg_col + 1]),
               waits=[mm_t], inc="se")
            sl_t = tok("se")
            ev("vector", lambda e, ku=ku, tt=tt, bi=bi: e.tensor_tensor(
                vstage[bi][:, tt * 512:(tt + 1) * 512], silu_t[tt % 2][:, :],
                pgu[ku][:, :], Alu.mult),
               waits=[sl_t], inc="ve")
            ev("vector", lambda e, tt=tt, bi=bi: e.tensor_reduce(
                vm8[:, tt:tt + 1], vstage[bi][:, tt * 512:(tt + 1) * 512],
                axis=AX, op=Alu.max, apply_absolute_value=True), inc="ve")
            pg_free[tt % 2] = tok("ve")
            vred.append(tok("ve"))
        silu_done[ic] = tok("se")
        ev("vector", lambda e: e.tensor_reduce(ssc[:, 14:15], vm8[:, :], axis=AX,
                                               op=Alu.max), waits=[vred[-1]], inc="ve")
        _t = tok("ve")
        ev("vector", lambda e, ic=ic: e.tensor_scalar(
            vmax_buf[:, ic:ic + 1], ssc[:, 14:15], wsu_sc[:, ic:ic + 1], None,
            Alu.mult), waits=[_t], inc="ve")
        v_t = tok("ve")
        sps = "spa" if ic % 2 == 0 else "spb"
        ev("gpsimd", lambda e, bi=bi, ic=ic: e.dma_start(out=vd[ic, :, :],
                                                         in_=vstage[bi][:, :]),
           waits=[v_t], inc=sps, n=16)
        spill_t[ic] = tok(sps)

    if "xint" in dbg:
        ev("gpsimd", lambda e: e.dma_start(out=dbg["xint"][:, :], in_=xint[:, :]),
           waits=[tok("ve")], inc="gld", n=16)

    p.barrier()
    p.flush()
    ph.close()

    ph = ExitStack()
    pfold = pst("pfold2", [128, 512], F32, ph)
    wdT = sbt("wdT", [128, 22528], BF16, per)
    wds = [sbt(f"wds_{i}", [128, 2816], F32, ph) for i in range(2)]
    wdi32 = sbt("wdi32", [128, 2816], I32, ph)
    wdbf = sbt("wdbf", [128, 2816], BF16, ph)
    ptw2 = pst("ptw2", [128, 1024], BF16, ph)

    def fold2(buf, ncols, cc_i, cc_o, clip_col, invb):
        ev("vector", lambda e: e.tensor_reduce(ssc[:, 8:9], buf[:, 0:ncols], axis=AX,
                                               op=Alu.max, apply_absolute_value=True),
           inc="ve")
        f1 = tok("ve")
        ev("tensor", lambda e: e.transpose(pfold[0:1, 0:128], ssc[:, 8:9], identf[:, :]),
           waits=[f1])
        ev("tensor", lambda e: e.drain(), inc="pe")
        f2 = tok("pe")
        ev("vector", lambda e: e.tensor_reduce(srow[0:1, 0:1], pfold[0:1, 0:128],
                                               axis=AX, op=Alu.max,
                                               apply_absolute_value=True),
           waits=[f2], inc="ve")
        f3 = tok("ve")
        ev("sync", lambda e, cc_i=cc_i: e.dma_start(out=cc_i[0:1, 0:1],
                                                    in_=srow[0:1, 0:1]),
           waits=[f3], inc="ld", n=16)
        f4 = tok("ld")
        ev("gpsimd", lambda e, cc_i=cc_i, cc_o=cc_o: e.collective_compute(
            "AllGather", Alu.bypass, replica_groups=[list(range(NCORE))],
            ins=[cc_i.ap().opt()], outs=[cc_o.ap().opt()]),
           waits=[f4], inc="cc")
        f5 = tok("cc")
        ev("gpsimd", lambda e, cc_o=cc_o: e.dma_start(
            out=srow[0:1, 128:256], in_=cc_o[:, :]), waits=[f5], inc="gld", n=16)
        f6 = tok("gld")
        ev("vector", lambda e: e.tensor_reduce(
            srow[0:1, 1:2],
            srow[0:1, 128:256].rearrange("o (r j) -> o r j", r=8)[:, :, 0:1]
            .rearrange("o r j -> o (r j)"),
            axis=AX, op=Alu.max, apply_absolute_value=True),
           waits=[f6], inc="ve")
        gmax_t = tok("ve")
        ev("scalar", lambda e, clip_col=clip_col: e.activation(
            srow[0:1, 2:3], clips_sb[0:1, clip_col:clip_col + 1], AF.Exp,
            bias=0.0, scale=fc[0:1, 4:5]), inc="se")
        sg_t = tok("se")
        ev("vector", lambda e: e.tensor_scalar(srow[0:1, 2:3], srow[0:1, 2:3],
                                               1.0, None, Alu.add),
           waits=[sg_t], inc="ve")
        _pt = tok("ve")
        ev("vector", lambda e: e.reciprocal(srow[0:1, 2:3], srow[0:1, 2:3]),
           waits=[_pt], inc="ve")
        _rt = tok("ve")
        ev("vector", lambda e: e.tensor_tensor(srow[0:1, 3:4], srow[0:1, 2:3],
                                               srow[0:1, 1:2], Alu.mult),
           waits=[_rt, gmax_t], inc="ve")
        _tt = tok("ve")
        ev("vector", lambda e: e.tensor_scalar(srow[0:1, 4:5], srow[0:1, 3:4],
                                               1.0 / 127.0, None, Alu.mult),
           waits=[_tt], inc="ve")
        ev("vector", lambda e: e.reciprocal(srow[0:1, 5:6], srow[0:1, 3:4]), inc="ve")
        _rt2 = tok("ve")
        ev("vector", lambda e: e.tensor_scalar(srow[0:1, 5:6], srow[0:1, 5:6],
                                               127.0, None, Alu.mult),
           waits=[_rt2], inc="ve")
        sc_t = tok("ve")
        ev("tensor", lambda e: e.matmul(pfold[:, 0:2], onesf[0:1, :], srow[0:1, 4:6],
                                        start=True, stop=True),
           waits=[sc_t], inc="pe")
        bc2 = tok("pe")
        ev("vector", lambda e, invb=invb: e.tensor_copy(invb[:, :], pfold[:, 0:2]),
           waits=[bc2], inc="ve")

    fold2(vmax_buf, NIC, cc2_in, cc2_out, 1, hinvb)
    hs_t = tok("ve")
    ev("vector", lambda e: e.tensor_scalar(qscale_g[:, 0:NIC], wsu_sc[:, 0:NIC],
                                           hinvb[:, 1:2], None, Alu.mult),
       waits=[hs_t], inc="ve")

    if "scales" in dbg:
        ev("sync", lambda e: e.dma_start(out=dbg["scales"][:, 0:24], in_=wsu_sc[:, :]),
           waits=[tok("ve")], inc="ld", n=16)
        ev("sync", lambda e: e.dma_start(out=dbg["scales"][:, 24:48], in_=qscale_g[:, :]),
           inc="ld", n=16)
        ev("sync", lambda e: e.dma_start(out=dbg["scales"][:, 48:50], in_=xinvb[:, :]),
           inc="ld", n=16)
        ev("sync", lambda e: e.dma_start(out=dbg["scales"][:, 50:52], in_=hinvb[:, :]),
           inc="ld", n=16)
        ev("sync", lambda e: e.dma_start(out=dbg["scales"][:, 52:60], in_=scd_g[:, :]),
           inc="ld", n=16)

    ev("sync", lambda e: e.dma_start(out=wds[0][:, :], in_=wd_in[0:128, :]),
       inc="lwa", n=16)
    wld = {0: tok("lwa")}
    tqf = {}
    for dcb in range(NDC):
        bi = dcb % 2
        if dcb + 1 < NDC:
            waits = []
            if (dcb - 1) in tqf:
                waits.append(tqf[dcb - 1])
            lws = "lwa" if (dcb + 1) % 2 == 0 else "lwb"
            ev("sync", lambda e, dcb=dcb, b2=(dcb + 1) % 2: e.dma_start(
                out=wds[b2][:, :], in_=wd_in[(dcb + 1) * 128:(dcb + 2) * 128, :]),
               waits=waits, inc=lws, n=16)
            wld[dcb + 1] = tok(lws)
        prev_tp = tok("pe")
        ev("vector", lambda e, bi=bi: e.tensor_reduce(
            ssc[:, 9:10], wds[bi][:, :], axis=AX, op=Alu.max,
            apply_absolute_value=True), waits=[wld[dcb], prev_tp], inc="ve")
        _t = tok("ve")
        ev("vector", lambda e: e.reciprocal(ssc[:, 10:11], ssc[:, 9:10]),
           waits=[_t], inc="ve")
        _t = tok("ve")
        ev("vector", lambda e: e.tensor_scalar(ssc[:, 10:11], ssc[:, 10:11],
                                               127.0, None, Alu.mult),
           waits=[_t], inc="ve")
        _t2 = tok("ve")
        ev("vector", lambda e, bi=bi: e.tensor_scalar(
            wdi32[:, :], wds[bi][:, :], ssc[:, 10:11], None, Alu.mult),
           waits=[_t2], inc="ve")
        ev("vector", lambda e: e.tensor_scalar(wdbf[:, :], wdi32[:, :],
                                               127.0, -128.0, Alu.min, Alu.max),
           inc="ve")
        ev("vector", lambda e: e.tensor_scalar(ssc[:, 11:12], ssc[:, 9:10],
                                               1.0 / 127.0, None, Alu.mult),
           waits=[_t], inc="ve")
        _t3 = tok("ve")
        ev("vector", lambda e, dcb=dcb: e.tensor_scalar(
            scd_g[:, dcb:dcb + 1], ssc[:, 11:12], hinvb[:, 0:1], None, Alu.mult),
           waits=[_t3], inc="ve")
        tqf[dcb] = tok("ve")
        q_done = tok("ve")
        done = 0
        prev_evac2 = None
        while done < NIC:
            grp = min(8, NIC - done)
            w_tp = [q_done] if prev_evac2 is None else [prev_evac2]
            for j in range(grp):
                icx = done + j
                ev("tensor", lambda e, icx=icx, j=j: e.transpose(
                    ptw2[:, j * 128:(j + 1) * 128],
                    wdbf[:, icx * 128:(icx + 1) * 128], identb[:, :]),
                   waits=(w_tp if j == 0 else ()))
            ev("tensor", lambda e: e.drain(), inc="pe")
            tp_t = tok("pe")
            ev("vector", lambda e, done=done, grp=grp, dcb=dcb: e.tensor_copy(
                wdT[:, :].rearrange("p (i c d) -> p i c d", c=8, d=128)
                [:, done:done + grp, dcb:dcb + 1, :],
                ptw2[:, 0:grp * 128].rearrange("p (i d) -> p i d", d=128)
                [:, :, None, :] if False else
                ptw2[:, 0:grp * 128].rearrange("p (i o d) -> p i o d", o=1, d=128)),
               waits=[tp_t], inc="ve")
            prev_evac2 = tok("ve")
            done += grp

    p.barrier()
    p.flush()
    ph.close()

    ph = ExitStack()
    hint = [sbt(f"hint_{i}", [128, 11264], BF16, ph) for i in range(2)]
    vrs = [sbt(f"vrs_{i}", [128, 512], F32, ph) for i in range(4)]
    vi32 = [sbt(f"vi32_{i}", [128, 512], I32, ph) for i in range(2)]
    x1r = [sbt(f"x1r_{i}", [128, 512], F32, ph) for i in range(2)]
    x2t = [sbt(f"x2t_{i}", [128, 4096], F32, ph) for i in range(2)]
    rb2 = sbt("rb2", [128, 4096], BF16, ph)
    rsq2 = [sbt(f"rsq2_{i}", [128, 512], BF16, ph) for i in range(3)]
    ostage = [sbt(f"ostage_{i}", [128, 1024], F32, ph) for i in range(2)]
    pd = [pst(f"pd_{i}", [128, 512], F32, ph) for i in range(2)]
    pm = pst("pm7", [1, 512], F32, ph)
    ps2 = pst("ps27", [1, 512], F32, ph)
    pbc_nm = pst("pbc_nm7", [128, 512], F32, ph)
    pbc_rs = pst("pbc_rs7", [128, 512], F32, ph)
    ptr7 = [pst(f"ptr7_{i}", [128, 512], F32, ph) for i in range(2)]

    hq_t = {}
    pd_free = [None, None]
    bc_free = None
    stats_pend7 = []
    out_pend = []
    tr_bank = {}
    x2_freed = {}

    def flush_out_group():
        xb2, tt0, tc, g, ln_tok = out_pend.pop(0)
        gb = (tt0 * 8 + tc * 2 + g) % 2
        ob = tc % 2
        w_tr = [ln_tok]
        if gb in tr_bank:
            w_tr.append(tr_bank[gb])
        if g == 0 and ost_free[ob] is not None:
            w_tr.append(ost_free[ob])
        for j in range(4):
            dc = g * 4 + j
            ev("tensor", lambda e, gb=gb, dc=dc, tc=tc, j=j, xb2=xb2: e.transpose(
                ptr7[gb][:, j * 128:(j + 1) * 128],
                x2t[xb2][:, dc * 512 + tc * 128: dc * 512 + (tc + 1) * 128],
                identf[:, :]),
               waits=(w_tr if j == 0 else ()))
        ev("tensor", lambda e: e.drain(), inc="pe")
        tp_t = tok("pe")
        ev("vector", lambda e, gb=gb, g=g, ob=ob: e.tensor_copy(
            ostage[ob][:, g * 512:(g + 1) * 512], ptr7[gb][:, :]),
           waits=[tp_t], inc="ve")
        tr_bank[gb] = tok("ve")
        if g == 1:
            sts = "sta" if ob == 0 else "stb"
            ev("sync", lambda e, ob=ob, tt0=tt0, tc=tc: e.dma_start(
                out=out[tt0 * 512 + tc * 128: tt0 * 512 + (tc + 1) * 128, :],
                in_=ostage[ob][:, :]),
               waits=[tr_bank[gb]], inc=sts, n=16)
            ost_free[ob] = tok(sts)
        if tc == 3 and g == 1:
            x2_freed[xb2] = tok("ve")

    def flush_stats7(bcf):
        for (dcx, ct2, sqt) in stats_pend7:
            w_m1 = [ct2]
            if dcx == 0 and bcf is not None:
                w_m1.append(bcf)
            ev("tensor", lambda e, dcx=dcx: e.matmul(
                pm[:, :], onesb[:, 0:1], rb2[:, dcx * 512:(dcx + 1) * 512],
                start=(dcx == 0), stop=(dcx == NDC - 1)), waits=w_m1, inc=None)
            ev("tensor", lambda e, dcx=dcx: e.matmul(
                ps2[:, :], onesb[:, 0:1], rsq2[dcx % 3][:, :],
                start=(dcx == 0), stop=(dcx == NDC - 1)), waits=[sqt], inc="pe")
        stats_pend7.clear()
    ost_free = [None, None]
    stt_t = {}
    mmdone_t = {}

    p.serialize = "p7" in KSER
    for tt in range(NTT):
        hb = tt % 2
        for ic in range(NIC):
            vb = (tt * NIC + ic) % 4
            waits = []
            if (tt * NIC + ic) >= 4:
                waits.append(hq_t.get((tt * NIC + ic - 4), tok("ve")))
            if tt >= 2 and ic == 0:
                waits.append(hq_t[(tt - 2, "done")])
            lvs = f"lv{(tt * NIC + ic) % 4}"
            ev("sync", lambda e, ic=ic, tt=tt, vb=vb: e.dma_start(
                out=vrs[vb][:, :], in_=vd[ic, :, tt * 512:(tt + 1) * 512]),
               waits=waits, inc=lvs, n=16)
            ld_t = tok(lvs)
            w2 = [ld_t]
            if (tt * NIC + ic) >= 2:
                w2.append(hq_t.get((tt * NIC + ic - 2), tok("ve")))
            if tt >= 2 and ic == 0:
                w2.append(mmdone_t[(tt - 2)] if (tt - 2) in mmdone_t else tok("pe"))
            vib = ic % 2
            ev("vector", lambda e, ic=ic, vb=vb, vib=vib: e.tensor_scalar(
                vi32[vib][:, :], vrs[vb][:, :], qscale_g[:, ic:ic + 1], None,
                Alu.mult), waits=w2, inc="ve")
            vq_t = tok("ve")
            clip_eng = "gpsimd" if ic % 2 == 0 else "vector"
            ev(clip_eng, lambda e, ic=ic, hb=hb, vib=vib: e.tensor_scalar(
                hint[hb][:, ic * 512:(ic + 1) * 512], vi32[vib][:, :],
                127.0, -128.0, Alu.min, Alu.max),
               waits=[vq_t], inc=("ge" if ic % 2 == 0 else "ve"))
            hq_t[(tt * NIC + ic)] = tok("ge" if ic % 2 == 0 else "ve")
        hq_done = (tok("ge"), tok("ve"))
        hq_t[(tt, "done")] = hq_done[0]
        hint_rdy = hq_done

        for dc in range(NDC):
            gi = tt * NDC + dc
            xb = gi % 2
            waits = []
            if gi >= 2:
                waits.append(stt_t[gi - 2])
            lxs = "lxa" if gi % 2 == 0 else "lxb"
            ev("sync", lambda e, dc=dc, tt=tt, xb=xb: e.dma_start(
                out=x1r[xb][:, :], in_=x1d[dc, :, tt * 512:(tt + 1) * 512]),
               waits=waits, inc=lxs, n=16)
            x1ld = tok(lxs)
            k = dc % 2
            w_mm = list(hint_rdy)
            if pd_free[k] is not None:
                w_mm.append(pd_free[k])
            for ic in range(NIC):
                ev("tensor", lambda e, k=k, ic=ic, dc=dc, hb=hb: e.matmul(
                    pd[k][:, :], wdT[:, (ic * 8 + dc) * 128:(ic * 8 + dc + 1) * 128],
                    hint[hb][:, ic * 512:(ic + 1) * 512],
                    start=(ic == 0), stop=(ic == NIC - 1)),
                   waits=(w_mm if ic == 0 else ()),
                   inc=("pe" if ic == NIC - 1 else None))
            mm_t = tok("pe")
            flush_stats7(bc_free)
            if out_pend:
                flush_out_group()
            w_stt = [mm_t, x1ld]
            if dc == 0 and (tt % 2) in x2_freed:
                w_stt.append(x2_freed[tt % 2])
            ev("vector", lambda e, k=k, dc=dc, xb=xb, xb2=tt % 2: e.scalar_tensor_tensor(
                x2t[xb2][:, dc * 512:(dc + 1) * 512], pd[k][:, :], scd_g[:, dc:dc + 1],
                x1r[xb][:, :], Alu.mult, Alu.add),
               waits=w_stt, inc="ve")
            pd_free[k] = tok("ve")
            stt_t[gi] = tok("ve")
            r2_t = tok("ve")
            ev("scalar", lambda e, dc=dc, xb2=tt % 2: e.copy(
                rb2[:, dc * 512:(dc + 1) * 512], x2t[xb2][:, dc * 512:(dc + 1) * 512]),
               waits=[r2_t], inc="se")
            cast_t = tok("se")
            ev("gpsimd", lambda e, dc=dc: e.tensor_tensor(
                rsq2[dc % 3][:, :], rb2[:, dc * 512:(dc + 1) * 512],
                rb2[:, dc * 512:(dc + 1) * 512], Alu.mult),
               waits=([cast_t] if dc < 3 else [cast_t, tok("pe")]), inc="ge")
            sq_t = tok("ge")
            stats_pend7.append((dc, cast_t, sq_t))
        flush_stats7(bc_free)
        mmdone_t[tt] = tok("pe")
        st_mm = tok("pe")
        ev("vector", lambda e: e.tensor_scalar(srow[0:1, 0:512], pm[:, :],
                                               1.0 / D, None, Alu.mult),
           waits=[st_mm], inc="ve")
        ev("vector", lambda e: e.tensor_tensor(srow[0:1, 512:1024], srow[0:1, 0:512],
                                               srow[0:1, 0:512], Alu.mult), inc="ve")
        ev("vector", lambda e: e.scalar_tensor_tensor(
            srow[0:1, 1024:1536], ps2[:, :], 1.0 / D, srow[0:1, 512:1024],
            Alu.mult, Alu.subtract), inc="ve")
        ev("vector", lambda e: e.tensor_scalar(srow[0:1, 1536:2048], srow[0:1, 0:512],
                                               -1.0, None, Alu.mult), inc="ve")
        var_t = tok("ve")
        ev("scalar", lambda e: e.activation(srow[0:1, 1024:1536], srow[0:1, 1024:1536],
                                            AF.Ln, bias=fc[0:1, 1:2], scale=1.0),
           waits=[var_t], inc="se")
        ev("scalar", lambda e: e.activation(srow[0:1, 1024:1536], srow[0:1, 1024:1536],
                                            AF.Exp, bias=0.0, scale=fc[0:1, 3:4]), inc="se")
        rstd_t = tok("se")
        ev("tensor", lambda e: e.matmul(pbc_nm[:, :], onesf[0:1, :],
                                        srow[0:1, 1536:2048], start=True, stop=True),
           waits=[rstd_t], inc="pe")
        ev("tensor", lambda e: e.matmul(pbc_rs[:, :], onesf[0:1, :],
                                        srow[0:1, 1024:1536], start=True, stop=True),
           inc="pe")
        bct = tok("pe")
        ln_t = {}
        for dc in range(NDC):
            ev("vector", lambda e, dc=dc, xb2=tt % 2: e.tensor_tensor(
                x2t[xb2][:, dc * 512:(dc + 1) * 512],
                x2t[xb2][:, dc * 512:(dc + 1) * 512],
                pbc_nm[:, :], Alu.add), waits=[bct] if dc == 0 else (), inc=None)
            ev("vector", lambda e, dc=dc, xb2=tt % 2: e.tensor_tensor(
                x2t[xb2][:, dc * 512:(dc + 1) * 512],
                x2t[xb2][:, dc * 512:(dc + 1) * 512],
                pbc_rs[:, :], Alu.mult), inc="ve")
            ln_t[dc] = tok("ve")
        bc_free = tok("ve")

        lnt_last = ln_t[NDC - 1]
        for tc in range(4):
            for g in range(2):
                out_pend.append((tt % 2, tt, tc, g, lnt_last))
        if tt == NTT - 1:
            while out_pend:
                flush_out_group()

    ev("sync", lambda e: None, waits=(("sta", cnt["sta"]), ("stb", cnt["stb"]),
                                      ("st", cnt["st"])))
    p.barrier()
    p.flush()
    ph.close()
    per.close()
    p.es.close()
    return nc


def _consts():
    identf = np.eye(128, dtype=np.float32)
    identb = np.eye(128, dtype=bf)
    onesf = np.ones((128, 128), dtype=np.float32)
    onesb = np.ones((128, 8), dtype=bf)
    hsel = np.zeros((128, 128), dtype=np.float32)
    for hp in range(NHC):
        for j in range(16):
            if j // 2 == hp:
                hsel[(j % 2) * 64:(j % 2 + 1) * 64, hp * 16 + j] = 1.0
    hsel = hsel.astype(bf)
    maskb = (np.arange(128)[None, :] >= np.arange(128)[:, None]).astype(bf)
    fconst = np.zeros((128, 8), dtype=np.float32)
    fconst[:, 0] = 1e-20
    fconst[:, 1] = 1e-5
    fconst[:, 2] = 0.5
    fconst[:, 3] = -0.5
    fconst[:, 4] = -1.0
    return identf, identb, onesf, onesb, hsel, maskb, fconst


_NC_CACHE = {}


def kernel(L0_attn_x, L0_attn_Wq, L0_attn_Wk, L0_attn_Wv, L0_attn_Wout,
           L0_ffn_clip_x, L0_ffn_clip_down, L0_ffn_w_gate, L0_ffn_w_up,
           L0_ffn_w_down):
    x = np.ascontiguousarray(np.asarray(L0_attn_x, dtype=np.float32))
    wq = np.ascontiguousarray(np.asarray(L0_attn_Wq, dtype=np.float32).reshape(D, D))
    wk = np.ascontiguousarray(np.asarray(L0_attn_Wk, dtype=np.float32).reshape(D, D))
    wv = np.ascontiguousarray(np.asarray(L0_attn_Wv, dtype=np.float32).reshape(D, D))
    wo = np.ascontiguousarray(np.asarray(L0_attn_Wout, dtype=np.float32).reshape(D, D))
    wg = np.ascontiguousarray(np.asarray(L0_ffn_w_gate, dtype=np.float32))
    wu = np.ascontiguousarray(np.asarray(L0_ffn_w_up, dtype=np.float32))
    wd = np.ascontiguousarray(np.asarray(L0_ffn_w_down, dtype=np.float32))
    clips = np.array([[float(np.asarray(L0_ffn_clip_x).reshape(-1)[0]),
                       float(np.asarray(L0_ffn_clip_down).reshape(-1)[0])]],
                     dtype=np.float32)
    identf, identb, onesf, onesb, hsel, maskb, fconst = _consts()

    if "nc" not in _NC_CACHE:
        _NC_CACHE["nc"] = build()
    nc = _NC_CACHE["nc"]

    base = {"wq": wq, "wk": wk, "wv": wv, "wo": wo, "wg": wg, "wu": wu,
            "wd": wd, "clips": clips, "identf": identf, "identb": identb,
            "onesf": onesf, "onesb": onesb, "hsel": hsel, "maskb": maskb,
            "fconst": fconst}
    in_maps = []
    for c in range(NCORE):
        m = dict(base)
        m["x"] = np.ascontiguousarray(
            x.reshape(B, L, D)[2 * c:2 * c + 2].reshape(T, D))
        in_maps.append(m)
    res = run_bass_kernel_spmd(nc, in_maps, core_ids=list(range(NCORE)))
    kernel._last_results = res.results
    outs = [res.results[c]["out"] for c in range(NCORE)]
    return np.concatenate(outs, axis=0).reshape(B, L, D).astype(np.float32)
